# version 06
# baseline (speedup 1.0000x reference)
import math
import numpy as np
import concourse.bass as bass
import concourse.mybir as mybir
from concourse.bass_utils import run_bass_kernel_spmd
from concourse.alu_op_type import AluOpType
from contextlib import ExitStack

F32 = mybir.dt.float32
BF16 = mybir.dt.bfloat16
F8 = mybir.dt.float8e4
I16 = mybir.dt.int16
I32 = mybir.dt.int32
AF = mybir.ActivationFunctionType
AX = mybir.AxisListType

S = 2048
D = 1024
H, KV, DH = 16, 4, 64
DHID = 4096
NT = S // 128
EPS_RMS = 1e-6
EPS_LN = 1e-5
ROPE_BASE = 10000.0
TQ = 512
NQTR = S // TQ
FC = 256
NFC = S // FC
NCT = DHID // 128

ENGS = ("sync", "tensor", "vector", "scalar", "gpsimd")
TWO_PI = 2.0 * math.pi
RC1 = float(np.float32(TWO_PI))
RC2 = float(np.float32(TWO_PI - RC1))
RC3 = float(np.float32(TWO_PI - RC1 - RC2))


class Prog:

    def __init__(self):
        self.items = {e: [] for e in ENGS}
        self.cnt = {s: 0 for s in ("LD", "G", "T", "V", "SC")}

    def op(self, eng, fn, inc=None):
        if inc is not None:
            self.cnt[inc[0]] += inc[1]
        self.items[eng].append((fn, inc))
        return self.cnt[inc[0]] if inc is not None else None

    def wait(self, eng, semname, val):
        if val <= 0:
            return
        assert val <= self.cnt[semname], (eng, semname, val, self.cnt[semname])
        self.items[eng].append(
            (lambda e, s, _sn=semname, _v=val: e.wait_ge(s[_sn], _v), None))

    def barrier(self):
        snap = dict(self.cnt)
        prod = {"sync": "LD", "gpsimd": "G", "tensor": "T", "vector": "V",
                "scalar": "SC"}
        for eng in ENGS:
            for sem, v in snap.items():
                if prod[eng] != sem:
                    self.wait(eng, sem, v)

    def realize(self, block, sems):
        def mk(eng):
            def body(e):
                for fn, inc in self.items[eng]:
                    r = fn(e, sems)
                    if inc is not None:
                        r.then_inc(sems[inc[0]], inc[1])
            return body
        for eng in ENGS:
            getattr(block, eng)(mk(eng))


def build_nc():
    nc = bass.Bass("TRN2", target_bir_lowering=False)

    x_in = nc.dram_tensor("x", [S, D], F32, kind="ExternalInput").ap()
    Wq = nc.dram_tensor("Wq", [D, D], F32, kind="ExternalInput").ap()
    Wk = nc.dram_tensor("Wk", [KV * DH, D], F32, kind="ExternalInput").ap()
    Wv = nc.dram_tensor("Wv", [KV * DH, D], F32, kind="ExternalInput").ap()
    Wo = nc.dram_tensor("Wo", [D, D], F32, kind="ExternalInput").ap()
    qnw = nc.dram_tensor("qnw", [1, DH], F32, kind="ExternalInput").ap()
    knw = nc.dram_tensor("knw", [1, DH], F32, kind="ExternalInput").ap()
    w1 = nc.dram_tensor("w1", [DHID, D], F32, kind="ExternalInput").ap()
    cw = nc.dram_tensor("cw", [DHID, 3], F32, kind="ExternalInput").ap()
    cb = nc.dram_tensor("cb", [1, DHID], F32, kind="ExternalInput").ap()
    w2 = nc.dram_tensor("w2", [D, DHID], F32, kind="ExternalInput").ap()
    y_out = nc.dram_tensor("y", [S, D], F32, kind="ExternalOutput").ap()
    _kind = {"kind": "ExternalOutput"} if DEBUG else {}
    if DEBUG:
        d_khi = nc.dram_tensor("d_khi", [128, KV, S], BF16, kind="ExternalOutput").ap()
        d_klo = nc.dram_tensor("d_klo", [128, KV, S], BF16, kind="ExternalOutput").ap()
        d_vext = nc.dram_tensor("d_vext", [128, NT * KV * (DH + 1)], F32, kind="ExternalOutput").ap()
        d_E = nc.dram_tensor("d_E", [128, TQ], F32, kind="ExternalOutput").ap()
        d_av = nc.dram_tensor("d_av", [128, 8, TQ], F32, kind="ExternalOutput").ap()
        d_S = nc.dram_tensor("d_S", [128, TQ], F32, kind="ExternalOutput").ap()
        d_qq = nc.dram_tensor("d_qq", [128, 8, TQ], BF16, kind="ExternalOutput").ap()
        d_qlo2 = nc.dram_tensor("d_qlo2", [128, 8, TQ], BF16, kind="ExternalOutput").ap()
        d_cos = nc.dram_tensor("d_cos", [128, NT, DH], F32, kind="ExternalOutput").ap()
        d_sin = nc.dram_tensor("d_sin", [128, NT, DH], F32, kind="ExternalOutput").ap()
        d_pos = nc.dram_tensor("d_pos", [128, NT], F32, kind="ExternalOutput").ap()
        d_t1 = nc.dram_tensor("d_t1", [128, 8, DHID], F8, kind="ExternalOutput").ap()
        d_t2 = nc.dram_tensor("d_t2", [128, NCT, D], F8, kind="ExternalOutput").ap()
        d_xint = nc.dram_tensor("d_xint", [128, 8, S], BF16, kind="ExternalOutput").ap()
        d_R = nc.dram_tensor("d_R", [128, S], F32, kind="ExternalOutput").ap()
        d_mw = nc.dram_tensor("d_mw", [128, 8], F32, kind="ExternalOutput").ap()
        d_y0 = nc.dram_tensor("d_y0", [128, NCT, FC], F32, kind="ExternalOutput").ap()
        d_o2 = nc.dram_tensor("d_o2", [128, D], F32, kind="ExternalOutput").ap()
        d_h = nc.dram_tensor("d_h", [128, FC + 2], F32, kind="ExternalOutput").ap()
        d_cv = nc.dram_tensor("d_cv", [128, FC], F32, kind="ExternalOutput").ap()
        d_cw = nc.dram_tensor("d_cw", [128, NCT, 3], F32, kind="ExternalOutput").ap()
        d_cb = nc.dram_tensor("d_cb", [128, NCT], F32, kind="ExternalOutput").ap()
        d_klo2 = nc.dram_tensor("d_klo2", [128, KV, S], BF16, kind="ExternalOutput").ap()
        d_E0 = nc.dram_tensor("d_E0", [128, TQ], F32, kind="ExternalOutput").ap()
    x1scr = nc.dram_tensor("x1scr", [S, D], F32, **_kind).ap()
    qscr_hi = nc.dram_tensor("qscr_hi", [128, 8, S], BF16, **_kind).ap()
    qscr_lo = nc.dram_tensor("qscr_lo", [128, 8, S], BF16, **_kind).ap()
    woscr = nc.dram_tensor("woscr", [128, 8, D], F32, **_kind).ap()

    p = Prog()

    def T(fn, inc=False):
        return p.op("tensor", fn, ("T", 1) if inc else None)

    def V(fn, inc=False):
        return p.op("vector", fn, ("V", 1) if inc else None)

    def SC(fn, inc=False):
        return p.op("scalar", fn, ("SC", 1) if inc else None)

    def LOAD(fn):
        return p.op("sync", fn, ("LD", 16))

    def STORE(fn):
        return p.op("gpsimd", fn, ("G", 16))

    def GOP(fn):
        return p.op("gpsimd", fn, ("G", 1))

    with ExitStack() as st:
        sb = lambda n, shp, dt: st.enter_context(nc.sbuf_tensor(n, shp, dt))
        ps = lambda n, shp, dt: st.enter_context(nc.psum_tensor(n, shp, dt))

        eye_f = sb("eye_f", [128, 128], F32)
        eye_b = sb("eye_b", [128, 128], BF16)
        ones1_64 = sb("ones1_64", [1, 64], F32)
        ones1_128 = sb("ones1_128", [1, 128], F32)
        ones128_f = sb("ones128_f", [128, 128], F32)
        io32 = sb("io32", [128, 128], I32)
        rxrow = sb("rxrow", [128, NT], F32)
        meanw1b = sb("meanw1b", [128, 1], F32)
        meanw2b = sb("meanw2b", [128, 1], F32)
        xch = [sb(f"xch{i}", [128, D], F32) for i in range(2)]
        c_epsln = sb("c_epsln", [128, 1], F32)
        c_epsrms = sb("c_epsrms", [128, 1], F32)
        stat = [sb(f"stat{i}", [128, 24], F32) for i in range(2)]
        sdump = sb("sdump", [128, 512], F32)

        bnk = [ps(f"b{i}", [128, 512], F32) for i in range(6)]
        pbf = [ps("pbf1", [128, 1024], BF16), ps("pbf2", [128, 1024], BF16)]
        lastread = {}

        def bank_guard(key):
            for dep in lastread.get(key, ()):
                p.wait("tensor", dep[0], dep[1])

        def bank_read(key, sem, vv):
            lastread[key] = [(sem, vv)]

        def bank_read_add(key, sem, vv):
            lastread.setdefault(key, []).append((sem, vv))

        sems = {}
        for nm in ("LD", "G", "T", "V", "SC"):
            sems[nm] = st.enter_context(nc.semaphore(nm))

        def ln_stats(sidx, src_fn, n_free, eps):
            stt = stat[sidx]
            half = n_free // 2
            V(lambda e, s, stt=stt: e.tensor_reduce(out=stt[:, 4:5], in_=src_fn(0, half), axis=AX.X, op=AluOpType.add))
            V(lambda e, s, stt=stt: e.tensor_reduce(out=stt[:, 5:6], in_=src_fn(half, n_free), axis=AX.X, op=AluOpType.add))
            V(lambda e, s: e.drain())
            V(lambda e, s, stt=stt: e.tensor_tensor(out=stt[:, 0:1], in0=stt[:, 4:5], in1=stt[:, 5:6], op=AluOpType.add), inc=True)
            p.wait("scalar", "V", p.cnt["V"])
            SC(lambda e, s, stt=stt: e.activation(out=sdump[:, 0:half], in_=src_fn(0, half), func=AF.Square, accum_out=stt[:, 6:7]))
            SC(lambda e, s, stt=stt: e.activation(out=sdump[:, 0:half], in_=src_fn(half, n_free), func=AF.Square, accum_out=stt[:, 7:8]), inc=True)
            p.wait("vector", "SC", p.cnt["SC"])
            V(lambda e, s, stt=stt: e.tensor_tensor(out=stt[:, 1:2], in0=stt[:, 6:7], in1=stt[:, 7:8], op=AluOpType.add))
            V(lambda e, s: e.drain())
            V(lambda e, s, stt=stt: e.tensor_scalar(out=stt[:, 12:13], in0=stt[:, 0:1], scalar1=1.0 / n_free, scalar2=None, op0=AluOpType.mult))
            V(lambda e, s: e.drain())
            V(lambda e, s, stt=stt: e.tensor_tensor(out=stt[:, 3:4], in0=stt[:, 12:13], in1=stt[:, 12:13], op=AluOpType.mult))
            V(lambda e, s, stt=stt: e.tensor_scalar(out=stt[:, 2:3], in0=stt[:, 1:2], scalar1=1.0 / n_free, scalar2=None, op0=AluOpType.mult))
            V(lambda e, s: e.drain())
            V(lambda e, s, stt=stt: e.tensor_tensor(out=stt[:, 13:14], in0=stt[:, 2:3], in1=stt[:, 3:4], op=AluOpType.subtract), inc=True)
            p.wait("scalar", "V", p.cnt["V"])
            SC(lambda e, s, stt=stt: e.activation(out=stt[:, 15:16], in_=stt[:, 13:14], func=AF.Sqrt, bias=c_epsln[:]), inc=True)
            p.wait("vector", "SC", p.cnt["SC"])
            V(lambda e, s, stt=stt: e.reciprocal(out=stt[:, 19:20], in_=stt[:, 15:16]))
            V(lambda e, s: e.drain())
            V(lambda e, s, stt=stt: e.tensor_tensor(out=stt[:, 20:21], in0=stt[:, 12:13], in1=stt[:, 19:20], op=AluOpType.mult))
            V(lambda e, s: e.drain())
            V(lambda e, s, stt=stt: e.tensor_scalar(out=stt[:, 21:22], in0=stt[:, 20:21], scalar1=-1.0, scalar2=None, op0=AluOpType.mult), inc=True)
            V(lambda e, s: e.drain())
            return p.cnt["V"]

        stA = st.enter_context(ExitStack())
        sbA = lambda n, shp, dt: stA.enter_context(nc.sbuf_tensor(n, shp, dt))
        kT_hi = sbA("kT_hi", [128, KV, S], BF16)
        kT_lo = sbA("kT_lo", [128, KV, S], BF16)
        v_ext = sbA("v_ext", [128, NT, KV, DH + 1], F32)
        with ExitStack() as st0:
            sb0 = lambda n, shp, dt: st0.enter_context(nc.sbuf_tensor(n, shp, dt))
            WT_hi = sb0("WT_hi", [128, 8, 1536], BF16)
            WT_lo = sb0("WT_lo", [128, 8, 1536], BF16)
            nrm = sb0("nrm", [1, 2 * DH], F32)
            invrep = sb0("invrep", [128, DH], F32)
            wq_rep = sb0("wq_rep", [128, DH], F32)
            wq_rot = sb0("wq_rot", [128, DH], F32)
            wk_rep = sb0("wk_rep", [128, DH], F32)
            wk_rot = sb0("wk_rot", [128, DH], F32)
            pos = sb0("pos", [128, NT], F32)
            ang = sb0("ang", [128, DH], F32)
            ang2 = sb0("ang2", [128, DH], F32)
            kf = sb0("kf", [128, DH], F32)
            k32s = sb0("k32s", [128, DH], I32)
            tsc = sb0("tsc", [128, DH], F32)
            trig = sb0("trig", [128, DH], F32)
            cosq = sb0("cosq", [128, NT, DH], F32)
            sinq = sb0("sinq", [128, NT, DH], F32)
            cosk = sb0("cosk", [128, NT, DH], F32)
            sink = sb0("sink", [128, NT, DH], F32)

            LOAD(lambda e, s: e.dma_start(out=nrm[0:1, 0:DH], in_=qnw))
            ld_nrm = LOAD(lambda e, s: e.dma_start(out=nrm[0:1, DH:2 * DH], in_=knw))

            for fn_ in (AF.Exp, AF.Sin, AF.Sqrt, AF.Square, AF.Silu, AF.Identity, AF.Sigmoid):
                SC(lambda e, s, fn_=fn_: e.activation(out=sdump[:, 0:1], in_=sdump[:, 0:1], func=fn_))
            SC(lambda e, s: e.activation(out=sdump[:, 0:1], in_=sdump[:, 1:2], func=AF.Exp), inc=True)

            GOP(lambda e, s: e.iota(io32[:, 0:NT], pattern=[[128, NT]], base=-1, channel_multiplier=1))
            g_io = p.cnt["G"]
            p.wait("vector", "G", g_io)
            V(lambda e, s: e.tensor_copy(out=pos[:], in_=io32[:, 0:NT]))
            V(lambda e, s: e.drain())
            V(lambda e, s: e.tensor_scalar(out=pos[:], in0=pos[:], scalar1=0.0, scalar2=None, op0=AluOpType.max), inc=True)
            V(lambda e, s: e.drain())
            v_pos = p.cnt["V"]
            p.wait("gpsimd", "V", v_pos)
            GOP(lambda e, s: e.iota(io32[:], pattern=[[1, 128]], base=0, channel_multiplier=-1))
            g_io2 = p.cnt["G"]
            GOP(lambda e, s: e.iota(io32[0:1, 0:32], pattern=[[1, 32]], base=0, channel_multiplier=0))
            g_io3 = p.cnt["G"]
            p.wait("vector", "G", g_io2)
            p.wait("gpsimd", "V", v_pos)
            p.wait("vector", "G", g_io3)
            V(lambda e, s: e.tensor_copy(out=ang[0:1, 0:32], in_=io32[0:1, 0:32]))
            V(lambda e, s: e.drain())
            V(lambda e, s: e.tensor_copy(out=ang[0:1, 32:64], in_=ang[0:1, 0:32]), inc=True)
            v_invf = p.cnt["V"]
            p.wait("gpsimd", "V", v_invf)
            GOP(lambda e, s: e.iota(io32[:], pattern=[[1, 128]], base=0, channel_multiplier=-1))
            g_io4 = p.cnt["G"]
            p.wait("vector", "G", g_io4)
            V(lambda e, s: e.tensor_single_scalar(out=eye_f[:], in_=io32[:], scalar=0.0, op=AluOpType.is_equal))
            V(lambda e, s: e.drain())
            V(lambda e, s: e.tensor_copy(out=eye_b[:], in_=eye_f[:]))
            V(lambda e, s: e.memset(ones1_64[:], 1.0))
            V(lambda e, s: e.memset(c_epsln[:], EPS_LN))
            V(lambda e, s: e.memset(c_epsrms[:], EPS_RMS))
            V(lambda e, s: e.memset(ones1_128[:], 1.0))
            V(lambda e, s: e.memset(ones128_f[:], 1.0))
            V(lambda e, s: e.memset(v_ext[:][:, :, :, DH:DH + 1], 1.0), inc=True)
            v_const = p.cnt["V"]
            p.wait("scalar", "V", v_const)
            SC(lambda e, s: e.activation(out=ang2[0:1, :], in_=ang[0:1, :], func=AF.Exp,
                                         scale=-math.log(ROPE_BASE) / 32.0), inc=True)
            sc_invf = p.cnt["SC"]

            p.wait("tensor", "SC", sc_invf)
            p.wait("tensor", "LD", ld_nrm)
            T(lambda e, s: e.matmul(bnk[0][0:128, 0:DH], ones1_128[:], ang2[0:1, :], start=True, stop=True), inc=True)
            T(lambda e, s: e.matmul(bnk[1][0:128, 0:2 * DH], ones1_128[:], nrm[0:1, :], start=True, stop=True), inc=True)
            t_bc = p.cnt["T"]
            p.wait("vector", "T", t_bc)
            V(lambda e, s: e.tensor_copy(out=invrep[:], in_=bnk[0][0:128, 0:DH]))
            V(lambda e, s: e.tensor_copy(out=wq_rep[:], in_=bnk[1][0:128, 0:DH]))
            V(lambda e, s: e.tensor_copy(out=wk_rep[:], in_=bnk[1][0:128, DH:2 * DH]))
            V(lambda e, s: e.tensor_copy(out=wq_rot[:, 0:32], in_=wq_rep[:, 32:64]))
            V(lambda e, s: e.tensor_copy(out=wq_rot[:, 32:64], in_=wq_rep[:, 0:32]))
            V(lambda e, s: e.tensor_copy(out=wk_rot[:, 0:32], in_=wk_rep[:, 32:64]))
            V(lambda e, s: e.tensor_copy(out=wk_rot[:, 32:64], in_=wk_rep[:, 0:32]), inc=True)
            V(lambda e, s: e.drain())
            v_bc = p.cnt["V"]
            bank_read(0, "V", v_bc)
            bank_read(1, "V", v_bc)

            def range_reduce(src_ap):
                V(lambda e, s: e.tensor_scalar(out=k32s[:], in0=src_ap, scalar1=1.0 / TWO_PI, scalar2=None, op0=AluOpType.mult))
                V(lambda e, s: e.drain())
                V(lambda e, s: e.tensor_copy(out=kf[:], in_=k32s[:]))
                V(lambda e, s: e.drain())
                V(lambda e, s: e.tensor_scalar(out=tsc[:], in0=kf[:], scalar1=RC1, scalar2=None, op0=AluOpType.mult))
                V(lambda e, s: e.drain())
                V(lambda e, s: e.tensor_tensor(out=trig[:], in0=src_ap, in1=tsc[:], op=AluOpType.subtract))
                V(lambda e, s: e.tensor_scalar(out=tsc[:], in0=kf[:], scalar1=RC2, scalar2=None, op0=AluOpType.mult))
                V(lambda e, s: e.drain())
                V(lambda e, s: e.tensor_tensor(out=trig[:], in0=trig[:], in1=tsc[:], op=AluOpType.subtract))
                V(lambda e, s: e.tensor_scalar(out=tsc[:], in0=kf[:], scalar1=RC3, scalar2=None, op0=AluOpType.mult))
                V(lambda e, s: e.drain())
                V(lambda e, s: e.tensor_tensor(out=trig[:], in0=trig[:], in1=tsc[:], op=AluOpType.subtract), inc=True)

            for tc in range(NT):
                V(lambda e, s, tc=tc: e.tensor_scalar(out=ang[:], in0=invrep[:], scalar1=pos[:, tc:tc + 1], scalar2=None, op0=AluOpType.mult))
                V(lambda e, s: e.drain())
                V(lambda e, s: e.tensor_scalar(out=ang2[:], in0=ang[:], scalar1=math.pi / 2.0, scalar2=None, op0=AluOpType.add), inc=True)
                range_reduce(ang[:])
                p.wait("scalar", "V", p.cnt["V"])
                sc_s = SC(lambda e, s: e.activation(out=tsc[:], in_=trig[:], func=AF.Sin), inc=True)
                p.wait("vector", "SC", sc_s)
                V(lambda e, s, tc=tc: e.tensor_tensor(out=sinq[:, tc, :], in0=tsc[:], in1=wq_rot[:], op=AluOpType.mult))
                V(lambda e, s, tc=tc: e.tensor_tensor(out=sink[:, tc, :], in0=tsc[:], in1=wk_rot[:], op=AluOpType.mult), inc=True)
                range_reduce(ang2[:])
                p.wait("scalar", "V", p.cnt["V"])
                sc_c = SC(lambda e, s: e.activation(out=tsc[:], in_=trig[:], func=AF.Sin), inc=True)
                p.wait("vector", "SC", sc_c)
                V(lambda e, s, tc=tc: e.tensor_tensor(out=cosq[:, tc, :], in0=tsc[:], in1=wq_rep[:], op=AluOpType.mult))
                V(lambda e, s, tc=tc: e.tensor_tensor(out=cosk[:, tc, :], in0=tsc[:], in1=wk_rep[:], op=AluOpType.mult), inc=True)

            with ExitStack() as stW1:
                wostg = stW1.enter_context(nc.sbuf_tensor("wostg", [128, 8, D], F32))
                wot = stW1.enter_context(nc.sbuf_tensor("wot", [128, D], F32))
                ld_wo = LOAD(lambda e, s: e.dma_start(out=wostg[:], in_=Wo.rearrange("(a p) d -> p a d", p=128)))
                p.wait("tensor", "LD", ld_wo)
                g_wot = 0
                for m in range(8):
                    for dk in range(8):
                        bi = dk % 2
                        bank_guard(bi)
                        T(lambda e, s, m=m, dk=dk, bi=bi:
                          e.transpose(bnk[bi][:, 0:128], wostg[:, dk, m * 128:(m + 1) * 128], eye_f[:]), inc=True)
                        p.wait("vector", "T", p.cnt["T"])
                        p.wait("vector", "G", g_wot)
                        V(lambda e, s, dk=dk, bi=bi:
                          e.tensor_copy(out=wot[:, dk * 128:(dk + 1) * 128], in_=bnk[bi][:, 0:128]), inc=True)
                        bank_read(bi, "V", p.cnt["V"])
                    v_wot = p.cnt["V"]
                    p.wait("gpsimd", "V", v_wot)
                    g_wot = STORE(lambda e, s, m=m: e.dma_start(out=woscr[:, m, :], in_=wot[:]))
                t_wo_done = p.cnt["T"]
                g_wo_done = p.cnt["G"]

            stW2 = st0.enter_context(ExitStack())
            wstg = stW2.enter_context(nc.sbuf_tensor("wstg", [128, 12, D], F32))
            p.wait("sync", "T", t_wo_done)
            p.wait("sync", "G", g_wo_done)
            LOAD(lambda e, s: e.dma_start(out=wstg[:, 0:8, :], in_=Wq.rearrange("(a p) d -> p a d", p=128)))
            LOAD(lambda e, s: e.dma_start(out=wstg[:, 8:10, :], in_=Wk.rearrange("(a p) d -> p a d", p=128)))
            ld_wqkv = LOAD(lambda e, s: e.dma_start(out=wstg[:, 10:12, :], in_=Wv.rearrange("(a p) d -> p a d", p=128)))

            if DEBUG:
                p.wait("gpsimd", "V", p.cnt["V"])
                STORE(lambda e, s: e.dma_start(out=d_cos, in_=cosq[:]))
                STORE(lambda e, s: e.dma_start(out=d_sin, in_=sinq[:]))
                STORE(lambda e, s: e.dma_start(out=d_pos, in_=pos[:]))
            p.wait("tensor", "LD", ld_wqkv)
            for dk in range(8):
                for jj in (0, 4, 8):
                    bi = (dk + jj // 4) % 2
                    bank_guard(bi)
                    for j2 in range(4):
                        T(lambda e, s, j=jj + j2, dk=dk, bi=bi, j2=j2:
                          e.transpose(bnk[bi][:, j2 * 128:(j2 + 1) * 128], wstg[:, j, dk * 128:(dk + 1) * 128], eye_f[:]),
                          inc=(j2 == 3))
                    p.wait("vector", "T", p.cnt["T"])
                    V(lambda e, s, dk=dk, jj=jj, bi=bi:
                      e.tensor_copy(out=WT_hi[:, dk, jj * 128:(jj + 4) * 128], in_=bnk[bi][:, 0:512]))
                    V(lambda e, s: e.drain())
                    V(lambda e, s, dk=dk, jj=jj, bi=bi:
                      e.tensor_tensor(out=WT_lo[:, dk, jj * 128:(jj + 4) * 128], in0=bnk[bi][:, 0:512],
                                      in1=WT_hi[:, dk, jj * 128:(jj + 4) * 128], op=AluOpType.subtract), inc=True)
                    bank_read(bi, "V", p.cnt["V"])
            stW2.close()
            a_hi = [sb0(f"a_hi{i}", [128, D], BF16) for i in range(2)]
            a_lo = [sb0(f"a_lo{i}", [128, D], BF16) for i in range(2)]
            a_f32 = [sb0(f"a_f32{i}", [128, D], F32) for i in range(2)]
            aT_hi = [sb0(f"aT_hi{i}", [128, 8, 128], BF16) for i in range(2)]
            aT_lo = [sb0(f"aT_lo{i}", [128, 8, 128], BF16) for i in range(2)]
            qa = sb0("qa", [128, 1280], F32)
            qb = sb0("qb", [128, 1280], F32)
            qc = sb0("qc", [128, 1280], F32)
            qd = sb0("qd", [128, 1280], F32)
            qsp_hi = sb0("qsp_hi", [128, 1280], BF16)
            qsp_lo = sb0("qsp_lo", [128, 1280], BF16)
            qTs = [sb0(f"qTs{i}", [128, 8, 128], BF16) for i in range(4)]
            ss20 = sb0("ss20", [128, 20], F32)
            u20 = sb0("u20", [128, 20], F32)
            r20 = sb0("r20", [128, 20], F32)
            t20 = sb0("t20", [128, 20], F32)

            ev_done = [0, 0]
            ax_done = [0, 0]
            qts_st = [0, 0, 0, 0]

            for tc in range(NT):
                par = tc % 2
                bs = bnk[0:3] if par == 0 else bnk[3:6]
                bkeys = (0, 1, 2) if par == 0 else (3, 4, 5)

                if tc >= 2:
                    p.wait("sync", "V", ax_done[par])
                ld_x = LOAD(lambda e, s, tc=tc, par=par: e.dma_start(
                    out=xch[par][:], in_=x_in[tc * 128:(tc + 1) * 128, :]))

                p.wait("vector", "LD", ld_x)
                v_ln = ln_stats(par, (lambda lo, hi, par=par: xch[par][:, lo:hi]), D, EPS_LN)
                p.wait("scalar", "V", v_ln)
                SC(lambda e, s, par=par: e.activation(out=a_hi[par][:], in_=xch[par][:], func=AF.Identity,
                                                      scale=stat[par][:, 19:20], bias=stat[par][:, 21:22]))
                SC(lambda e, s, par=par: e.activation(out=a_f32[par][:], in_=xch[par][:], func=AF.Identity,
                                                      scale=stat[par][:, 19:20], bias=stat[par][:, 21:22]), inc=True)
                sc_app = p.cnt["SC"]
                p.wait("vector", "SC", sc_app)
                V(lambda e, s, par=par: e.tensor_tensor(out=a_lo[par][:], in0=a_f32[par][:], in1=a_hi[par][:], op=AluOpType.subtract), inc=True)
                v_asplit = p.cnt["V"]

                p.wait("tensor", "V", v_asplit)
                for src, dst in ((a_hi, aT_hi), (a_lo, aT_lo)):
                    pk = ("pbf", par)
                    bank = pbf[par]
                    bank_guard(pk)
                    for dk in range(8):
                        T(lambda e, s, src=src, par=par, dk=dk, bank=bank:
                          e.transpose(bank[:, dk * 128:(dk + 1) * 128], src[par][:, dk * 128:(dk + 1) * 128], eye_b[:]),
                          inc=(dk == 7))
                    p.wait("vector", "T", p.cnt["T"])
                    V(lambda e, s, dst=dst, par=par, bank=bank:
                      e.tensor_copy(out=dst[par][:], in_=bank[:].rearrange("p (a b) -> p a b", a=8)), inc=True)
                    bank_read(pk, "V", p.cnt["V"])
                v_aT = p.cnt["V"]

                for bi in bkeys:
                    bank_guard(bi)
                p.wait("tensor", "V", v_aT)
                first = [True, True, True]
                for pi, (lsrc, rsrc) in enumerate(((aT_hi, WT_hi), (aT_hi, WT_lo), (aT_lo, WT_hi))):
                    for dk in range(8):
                        for ns in range(3):
                            lastmm = (pi == 2 and dk == 7 and ns == 2)
                            T(lambda e, s, lsrc=lsrc, rsrc=rsrc, par=par, dk=dk, ns=ns, st_=first[ns], bs=bs:
                              e.matmul(bs[ns][:, 0:512], lsrc[par][:, dk, :], rsrc[:, dk, ns * 512:(ns + 1) * 512],
                                       start=st_, stop=True),
                              inc=lastmm)
                            first[ns] = False
                t_qkv = p.cnt["T"]

                p.wait("scalar", "T", t_qkv)
                for hseg in range(20):
                    col = hseg * 64
                    SC(lambda e, s, bs=bs, bi=col // 512, off=col % 512, hseg=hseg:
                       e.activation(out=sdump[:, 0:64], in_=bs[bi][:, off:off + 64], func=AF.Square,
                                    accum_out=ss20[:, hseg:hseg + 1]), inc=(hseg == 19))
                sc_ss = p.cnt["SC"]
                p.wait("vector", "SC", sc_ss)
                V(lambda e, s: e.tensor_scalar(out=u20[:], in0=ss20[:], scalar1=1.0 / 64.0, scalar2=None, op0=AluOpType.mult), inc=True)
                p.wait("scalar", "V", p.cnt["V"])
                SC(lambda e, s: e.activation(out=t20[:], in_=u20[:], func=AF.Sqrt, bias=c_epsrms[:]), inc=True)
                p.wait("vector", "SC", p.cnt["SC"])
                V(lambda e, s: e.reciprocal(out=r20[:], in_=t20[:]))
                V(lambda e, s: e.drain())
                for hseg in range(20):
                    col = hseg * 64
                    V(lambda e, s, bs=bs, bi=col // 512, off=col % 512, hseg=hseg, col=col:
                      e.tensor_scalar(out=qa[:, col:col + 64], in0=bs[bi][:, off:off + 64],
                                      scalar1=r20[:, hseg:hseg + 1], scalar2=None, op0=AluOpType.mult))
                V(lambda e, s, bs=bs, tc=tc:
                  e.tensor_copy(out=v_ext[:][:, tc, :, 0:DH],
                                in_=bs[2][:, 256:512].rearrange("p (b c) -> p b c", b=KV)), inc=True)
                ev_done[par] = p.cnt["V"]
                for bi in bkeys:
                    bank_read(bi, "V", ev_done[par])
                    bank_read_add(bi, "SC", sc_ss)

                for hh in range(16):
                    V(lambda e, s, hh=hh, tc=tc: e.tensor_copy(out=qd[:, hh * 64:(hh + 1) * 64], in_=cosq[:, tc, :]))
                for kk2 in range(4):
                    V(lambda e, s, kk2=kk2, tc=tc: e.tensor_copy(out=qd[:, 1024 + kk2 * 64:1024 + (kk2 + 1) * 64], in_=cosk[:, tc, :]))
                V(lambda e, s: e.drain())
                V(lambda e, s: e.tensor_tensor(out=qb[:], in0=qa[:], in1=qd[:], op=AluOpType.mult))
                V(lambda e, s: e.tensor_scalar(
                    out=qc[:].rearrange("p (a b) -> p a b", a=20)[:, :, 0:32],
                    in0=qa[:].rearrange("p (a b) -> p a b", a=20)[:, :, 32:64],
                    scalar1=-1.0, scalar2=None, op0=AluOpType.mult))
                V(lambda e, s: e.tensor_copy(
                    out=qc[:].rearrange("p (a b) -> p a b", a=20)[:, :, 32:64],
                    in_=qa[:].rearrange("p (a b) -> p a b", a=20)[:, :, 0:32]))
                V(lambda e, s: e.drain())
                for hh in range(16):
                    V(lambda e, s, hh=hh, tc=tc: e.tensor_copy(out=qd[:, hh * 64:(hh + 1) * 64], in_=sinq[:, tc, :]))
                for kk2 in range(4):
                    V(lambda e, s, kk2=kk2, tc=tc: e.tensor_copy(out=qd[:, 1024 + kk2 * 64:1024 + (kk2 + 1) * 64], in_=sink[:, tc, :]))
                V(lambda e, s: e.drain())
                V(lambda e, s: e.tensor_tensor(out=qa[:], in0=qc[:], in1=qd[:], op=AluOpType.mult))
                V(lambda e, s: e.tensor_tensor(out=qc[:], in0=qb[:], in1=qa[:], op=AluOpType.add))
                V(lambda e, s: e.tensor_copy(out=qsp_hi[:], in_=qc[:]))
                V(lambda e, s: e.drain())
                V(lambda e, s: e.tensor_tensor(out=qsp_lo[:], in0=qc[:], in1=qsp_hi[:], op=AluOpType.subtract), inc=True)
                ax_done[par] = p.cnt["V"]
                v_qsp = p.cnt["V"]

                for si, (src, qdst, kdst, scr_t) in enumerate(
                        ((qsp_hi, qTs[0 + 2 * par], kT_hi, qscr_hi),
                         (qsp_lo, qTs[1 + 2 * par], kT_lo, qscr_lo))):
                    pk = ("pbf", par)
                    bank = pbf[par]
                    p.wait("tensor", "V", v_qsp)
                    bank_guard(pk)
                    for dk in range(8):
                        T(lambda e, s, src=src, dk=dk, bank=bank:
                          e.transpose(bank[:, dk * 128:(dk + 1) * 128], src[:, dk * 128:(dk + 1) * 128], eye_b[:]),
                          inc=(dk == 7))
                    p.wait("vector", "T", p.cnt["T"])
                    p.wait("vector", "G", qts_st[si + 2 * par])
                    V(lambda e, s, qdst=qdst, bank=bank: e.tensor_copy(out=qdst[:], in_=bank[:].rearrange("p (a b) -> p a b", a=8)), inc=True)
                    bank_read(pk, "V", p.cnt["V"])
                    v_qT = p.cnt["V"]
                    p.wait("gpsimd", "V", v_qT)
                    qts_st[si + 2 * par] = STORE(lambda e, s, qdst=qdst, scr_t=scr_t, tc=tc:
                                                 e.dma_start(out=scr_t[:, :, tc * 128:(tc + 1) * 128], in_=qdst[:]))
                    bank_guard(pk)
                    for j in range(2):
                        T(lambda e, s, src=src, j=j, bank=bank:
                          e.transpose(bank[:, j * 128:(j + 1) * 128], src[:, 1024 + j * 128:1024 + (j + 1) * 128], eye_b[:]),
                          inc=(j == 1))
                    p.wait("vector", "T", p.cnt["T"])
                    for j in range(2):
                        for half in range(2):
                            kv = 2 * j + half
                            V(lambda e, s, kdst=kdst, kv=kv, tc=tc, bank=bank, j=j, half=half:
                              e.tensor_copy(out=kdst[0:64, kv, tc * 128:(tc + 1) * 128],
                                            in_=bank[half * 64:half * 64 + 64, j * 128:(j + 1) * 128]))
                            V(lambda e, s, kdst=kdst, kv=kv, tc=tc, bank=bank, j=j, half=half:
                              e.tensor_copy(out=kdst[64:128, kv, tc * 128:(tc + 1) * 128],
                                            in_=bank[half * 64:half * 64 + 64, j * 128:(j + 1) * 128]),
                              inc=(j == 1 and half == 1))
                    bank_read(pk, "V", p.cnt["V"])

            if DEBUG:
                p.wait("gpsimd", "V", p.cnt["V"])
                STORE(lambda e, s: e.dma_start(out=d_khi, in_=kT_hi[:]))
                STORE(lambda e, s: e.dma_start(out=d_klo, in_=kT_lo[:]))
                STORE(lambda e, s: e.dma_start(out=d_vext, in_=v_ext[:].rearrange("p a b c -> p (a b c)")))
            p.barrier()

        lastread.clear()
        with ExitStack() as st2:
            sb2 = lambda n, shp, dt: st2.enter_context(nc.sbuf_tensor(n, shp, dt))
            Ebuf = sb2("Ebuf", [128, NT, TQ], F32)
            av_sb = sb2("av_sb", [128, 8, TQ], F32)
            qq_hi = [sb2(f"qq_hi{i}", [128, 8, TQ], BF16) for i in range(2)]
            qq_lo = [sb2(f"qq_lo{i}", [128, 8, TQ], BF16) for i in range(2)]
            wo_all = sb2("wo_all", [128, 8, D], F32)
            Zr = sb2("Zr", [1, TQ], F32)
            rZb = sb2("rZb", [64, TQ], F32)
            ln3 = sb2("ln3", [128, D], F32)
            yx1 = [sb2(f"yx1{i}", [128, D], F32) for i in range(2)]

            LOAD(lambda e, s: e.dma_start(out=wo_all[:], in_=woscr))
            ld_qq = {}
            for qtr in range(2):
                ld_qq[(qtr, 0)] = LOAD(lambda e, s, qtr=qtr: e.dma_start(out=qq_hi[qtr][:], in_=qscr_hi[:, :, qtr * TQ:(qtr + 1) * TQ]))
                ld_qq[(qtr, 1)] = LOAD(lambda e, s, qtr=qtr: e.dma_start(out=qq_lo[qtr][:], in_=qscr_lo[:, :, qtr * TQ:(qtr + 1) * TQ]))

            if DEBUG:
                p.wait("gpsimd", "LD", ld_qq[(0, 1)])
                STORE(lambda e, s: e.dma_start(out=d_qq, in_=qq_hi[0][:]))
            x1_st = [0, 0]
            for qtr in range(NQTR):
                qpar = qtr % 2
                qhi, qlo = qq_hi[qpar], qq_lo[qpar]
                p.wait("tensor", "LD", ld_qq[(qtr, 1)])
                for h in range(H):
                    kv = h // 4
                    hb = (h % 2) * 64
                    for c in range(NT):
                        bi = c % 2
                        bank_guard(bi)
                        _passes = ((kT_hi, qhi), (kT_hi, qlo), (kT_lo, qhi))[:QK_PASSES]
                        for pi, (lsrc, rsrc) in enumerate(_passes):
                            T(lambda e, s, lsrc=lsrc, rsrc=rsrc, hb=hb, kv=kv, c=c, h=h, bi=bi,
                                     st_=(pi == 0), sp_=(pi == len(_passes) - 1):
                              e.matmul(bnk[bi][:, 0:TQ],
                                       lsrc[hb:hb + 64, kv, c * 128:(c + 1) * 128],
                                       rsrc[hb:hb + 64, h // 2, :],
                                       start=st_, stop=sp_),
                              inc=(pi == len(_passes) - 1))
                        t_qk = p.cnt["T"]
                        p.wait("scalar", "T", t_qk)
                        sc_e = SC(lambda e, s, c=c, bi=bi:
                                  e.activation(out=Ebuf[:, c, :], in_=bnk[bi][:, 0:TQ], func=AF.Exp), inc=True)
                        if DEBUG and qtr == 0 and h == 0 and c == 0:
                            p.wait("vector", "T", t_qk)
                            V(lambda e, s, bi=bi: e.tensor_copy(out=ln3[:, 0:TQ], in_=bnk[bi][:, 0:TQ]), inc=True)
                            p.wait("gpsimd", "V", p.cnt["V"])
                            STORE(lambda e, s: e.dma_start(out=d_S, in_=ln3[:, 0:TQ]))
                            p.wait("gpsimd", "SC", sc_e)
                            STORE(lambda e, s: e.dma_start(out=d_E0, in_=Ebuf[:, 0, :]))
                            STORE(lambda e, s: e.dma_start(out=d_qlo2, in_=qq_lo[0][:]))
                            STORE(lambda e, s: e.dma_start(out=d_klo2, in_=kT_lo[:]))
                            p.wait("tensor", "G", p.cnt["G"])
                            sc_e = p.cnt["SC"]
                            bank_read(bi, "V", p.cnt["V"])
                        bank_read(bi, "SC", sc_e)
                    sc_exp_end = p.cnt["SC"]
                    bank_guard(2)
                    for c in range(NT):
                        p.wait("tensor", "SC", sc_exp_end - (NT - 1 - c))
                        T(lambda e, s, c=c, kv=kv:
                          e.matmul(bnk[2][0:65, 0:TQ], v_ext[:][:, c, kv, :], Ebuf[:, c, :],
                                   start=(c == 0), stop=(c == NT - 1)),
                          inc=(c == NT - 1))
                    t_av = p.cnt["T"]
                    p.wait("vector", "T", t_av)
                    V(lambda e, s: e.tensor_copy(out=Zr[0:1, :], in_=bnk[2][64:65, 0:TQ]), inc=True)
                    v_zr = p.cnt["V"]
                    p.wait("tensor", "V", v_zr)
                    bank_guard(3)
                    T(lambda e, s: e.matmul(bnk[3][0:64, 0:TQ], ones1_64[:], Zr[0:1, :], start=True, stop=True), inc=True)
                    t_zb = p.cnt["T"]
                    p.wait("vector", "T", t_zb)
                    V(lambda e, s: e.reciprocal(out=rZb[:], in_=bnk[3][0:64, 0:TQ]), inc=True)
                    bank_read(3, "V", p.cnt["V"])
                    V(lambda e, s, hb=hb, h=h: e.tensor_tensor(out=av_sb[hb:hb + 64, h // 2, :], in0=bnk[2][0:64, 0:TQ], in1=rZb[:], op=AluOpType.mult), inc=True)
                    bank_read(2, "V", p.cnt["V"])
                v_avn = p.cnt["V"]
                if DEBUG and qtr == 0:
                    p.wait("gpsimd", "V", v_avn)
                    STORE(lambda e, s: e.dma_start(out=d_av, in_=av_sb[:]))
                    STORE(lambda e, s: e.dma_start(out=d_E, in_=Ebuf[:, 0, :]))

                p.wait("tensor", "V", v_avn)
                for ts in range(TQ // 128):
                    tg = qtr * (TQ // 128) + ts
                    par = tg % 2
                    for dep in lastread.get(("yx1", par), ()):
                        p.wait("sync", dep[0], dep[1])
                    ld_x2 = LOAD(lambda e, s, tg=tg, par=par: e.dma_start(out=xch[par][:], in_=x_in[tg * 128:(tg + 1) * 128, :]))
                    for dhf in range(2):
                        bank_guard(4 + dhf)
                        for m in range(8):
                            T(lambda e, s, m=m, ts=ts, dhf=dhf:
                              e.matmul(bnk[4 + dhf][:, 0:512], av_sb[:, m, ts * 128:(ts + 1) * 128],
                                       wo_all[:, m, dhf * 512:(dhf + 1) * 512],
                                       start=(m == 0), stop=(m == 7)),
                              inc=(m == 7))
                    t_wo = p.cnt["T"]
                    p.wait("vector", "T", t_wo)
                    v_ln3 = ln_stats(par, (lambda lo, hi: bnk[4][:, lo:hi] if hi <= 512 else bnk[5][:, lo - 512:hi - 512]), D, EPS_LN)
                    p.wait("scalar", "V", v_ln3)
                    for dhf in range(2):
                        SC(lambda e, s, dhf=dhf, par=par:
                           e.activation(out=ln3[:, dhf * 512:(dhf + 1) * 512], in_=bnk[4 + dhf][:, 0:512], func=AF.Identity,
                                        scale=stat[par][:, 19:20], bias=stat[par][:, 21:22]), inc=(dhf == 1))
                    sc_ln3 = p.cnt["SC"]
                    bank_read(4, "SC", sc_ln3)
                    bank_read(5, "SC", sc_ln3)
                    p.wait("vector", "SC", sc_ln3)
                    p.wait("vector", "LD", ld_x2)
                    p.wait("vector", "G", x1_st[par])
                    V(lambda e, s, par=par: e.tensor_tensor(out=yx1[par][:], in0=ln3[:], in1=xch[par][:], op=AluOpType.add), inc=True)
                    lastread[("yx1", par)] = [("V", p.cnt["V"])]
                    v_x1 = p.cnt["V"]
                    p.wait("gpsimd", "V", v_x1)
                    x1_st[par] = STORE(lambda e, s, tg=tg, par=par: e.dma_start(out=x1scr[tg * 128:(tg + 1) * 128, :], in_=yx1[par][:]))
                if qtr + 2 < NQTR:
                    p.wait("sync", "T", p.cnt["T"])
                    ld_qq[(qtr + 2, 0)] = LOAD(lambda e, s, q2=qtr + 2, qpar=qpar: e.dma_start(out=qq_hi[qpar][:], in_=qscr_hi[:, :, q2 * TQ:(q2 + 1) * TQ]))
                    ld_qq[(qtr + 2, 1)] = LOAD(lambda e, s, q2=qtr + 2, qpar=qpar: e.dma_start(out=qq_lo[qpar][:], in_=qscr_lo[:, :, q2 * TQ:(q2 + 1) * TQ]))

            p.barrier()

        stA.close()
        lastread.clear()
        with ExitStack() as st3:
            sb3 = lambda n, shp, dt: st3.enter_context(nc.sbuf_tensor(n, shp, dt))
            tern1T = sb3("tern1T", [128, 8, DHID], F8)
            tern2T = sb3("tern2T", [128, NCT, D], F8)
            xintT = sb3("xintT", [128, 8, S], BF16)
            Rfull = sb3("Rfull", [128, S], F32)
            cwt = sb3("cwt", [128, NCT, 3], F32)
            cbt = sb3("cbt", [128, NCT], F32)
            x1c = [sb3(f"x1c{i}", [128, D], F32) for i in range(2)]
            a2 = sb3("a2", [128, D], F32)
            xi16 = sb3("xi16", [128, D], I16)
            xib = sb3("xib", [128, D], BF16)
            qs = sb3("qs", [128, 8], F32)
            diag = sb3("diag", [128, 128], F32)
            stQ = st3.enter_context(ExitStack())
            sbQ = lambda n, shp, dt: stQ.enter_context(nc.sbuf_tensor(n, shp, dt))
            wst = [sbQ(f"wst{i}", [128, DHID], F32) for i in range(2)]
            wsum = sbQ("wsum", [128, 40], F32)
            wi16 = sbQ("wi16", [128, DHID], I16)
            wtb = sbQ("wtb", [128, DHID], BF16)

            for i in range(32):
                par = i % 2
                for dep in lastread.get(("wst", par), ()):
                    p.wait("sync", dep[0], dep[1])
                ld_w1 = LOAD(lambda e, s, i=i, par=par: e.dma_start(out=wst[par][:, 0:D], in_=w1[i * 128:(i + 1) * 128, :]))
                p.wait("vector", "LD", ld_w1)
                V(lambda e, s, i=i, par=par: e.tensor_reduce(out=wsum[:, i:i + 1], in_=wst[par][:, 0:D], axis=AX.X, op=AluOpType.add, apply_absolute_value=True), inc=True)
                lastread[("wst", par)] = [("V", p.cnt["V"])]
            for i in range(8):
                par = i % 2
                for dep in lastread.get(("wst", par), ()):
                    p.wait("sync", dep[0], dep[1])
                ld_w2 = LOAD(lambda e, s, i=i, par=par: e.dma_start(out=wst[par][:], in_=w2[i * 128:(i + 1) * 128, :]))
                p.wait("vector", "LD", ld_w2)
                V(lambda e, s, i=i, par=par: e.tensor_reduce(out=wsum[:, 32 + i:33 + i], in_=wst[par][:], axis=AX.X, op=AluOpType.add, apply_absolute_value=True), inc=True)
                lastread[("wst", par)] = [("V", p.cnt["V"])]
            V(lambda e, s: e.drain())
            V(lambda e, s: e.tensor_reduce(out=qs[:, 0:1], in_=wsum[:, 0:32], axis=AX.X, op=AluOpType.add))
            V(lambda e, s: e.tensor_reduce(out=qs[:, 1:2], in_=wsum[:, 32:40], axis=AX.X, op=AluOpType.add), inc=True)
            V(lambda e, s: e.drain())
            p.wait("tensor", "V", p.cnt["V"])
            bank_guard(0)
            T(lambda e, s: e.matmul(bnk[0][0:1, 0:2], ones128_f[:, 0:1], qs[:, 0:2], start=True, stop=True), inc=True)
            p.wait("vector", "T", p.cnt["T"])
            V(lambda e, s: e.tensor_copy(out=qs[0:1, 2:4], in_=bnk[0][0:1, 0:2]), inc=True)
            bank_read(0, "V", p.cnt["V"])
            V(lambda e, s: e.drain())
            V(lambda e, s: e.tensor_scalar(out=qs[0:1, 4:6], in0=qs[0:1, 2:4], scalar1=1.0 / (DHID * D), scalar2=1e-5, op0=AluOpType.mult, op1=AluOpType.max), inc=True)
            V(lambda e, s: e.drain())
            v_mean = p.cnt["V"]
            p.wait("tensor", "V", v_mean)
            bank_guard(1)
            T(lambda e, s: e.matmul(bnk[1][0:128, 0:2], ones1_128[:], qs[0:1, 4:6], start=True, stop=True), inc=True)
            p.wait("vector", "T", p.cnt["T"])
            V(lambda e, s: e.tensor_copy(out=meanw1b[:], in_=bnk[1][0:128, 0:1]))
            V(lambda e, s: e.tensor_copy(out=meanw2b[:], in_=bnk[1][0:128, 1:2]), inc=True)
            bank_read(1, "V", p.cnt["V"])
            V(lambda e, s: e.drain())
            V(lambda e, s: e.reciprocal(out=qs[:, 6:7], in_=meanw1b[:]))
            V(lambda e, s: e.reciprocal(out=qs[:, 7:8], in_=meanw2b[:]), inc=True)
            V(lambda e, s: e.drain())

            for i in range(32):
                par = i % 2
                for dep in lastread.get(("wst", par), ()):
                    p.wait("sync", dep[0], dep[1])
                ld_w1 = LOAD(lambda e, s, i=i, par=par: e.dma_start(out=wst[par][:, 0:D], in_=w1[i * 128:(i + 1) * 128, :]))
                p.wait("vector", "LD", ld_w1)
                V(lambda e, s, par=par: e.tensor_scalar(out=wi16[:, 0:D], in0=wst[par][:, 0:D], scalar1=qs[:, 6:7], scalar2=None, op0=AluOpType.mult))
                lastread[("wst", par)] = [("V", p.cnt["V"] + 1)]
                V(lambda e, s: e.tensor_scalar(out=wtb[:, 0:D], in0=wi16[:, 0:D], scalar1=-1.0, scalar2=1.0, op0=AluOpType.max, op1=AluOpType.min), inc=True)
                p.wait("tensor", "V", p.cnt["V"])
                pk = ("pbf", i % 2)
                bank = pbf[i % 2]
                bank_guard(pk)
                for dk in range(8):
                    T(lambda e, s, dk=dk, bank=bank: e.transpose(bank[:, dk * 128:(dk + 1) * 128], wtb[:, dk * 128:(dk + 1) * 128], eye_b[:]), inc=(dk == 7))
                p.wait("vector", "T", p.cnt["T"])
                V(lambda e, s, i=i, bank=bank: e.tensor_copy(
                    out=tern1T[:][:, :, i * 128:(i + 1) * 128],
                    in_=bank[:].rearrange("p (a c) -> p a c", a=8)), inc=True)
                bank_read(pk, "V", p.cnt["V"])
            for i in range(8):
                par = i % 2
                for dep in lastread.get(("wst", par), ()):
                    p.wait("sync", dep[0], dep[1])
                ld_w2 = LOAD(lambda e, s, i=i, par=par: e.dma_start(out=wst[par][:], in_=w2[i * 128:(i + 1) * 128, :]))
                p.wait("vector", "LD", ld_w2)
                V(lambda e, s, par=par: e.tensor_scalar(out=wi16[:], in0=wst[par][:], scalar1=qs[:, 7:8], scalar2=None, op0=AluOpType.mult))
                lastread[("wst", par)] = [("V", p.cnt["V"] + 1)]
                V(lambda e, s: e.tensor_scalar(out=wtb[:], in0=wi16[:], scalar1=-1.0, scalar2=1.0, op0=AluOpType.max, op1=AluOpType.min), inc=True)
                p.wait("tensor", "V", p.cnt["V"])
                for g4 in range(4):
                    pk = ("pbf", g4 % 2)
                    bank = pbf[g4 % 2]
                    bank_guard(pk)
                    for j2 in range(8):
                        ct = g4 * 8 + j2
                        T(lambda e, s, ct=ct, bank=bank, j2=j2: e.transpose(bank[:, j2 * 128:(j2 + 1) * 128], wtb[:, ct * 128:(ct + 1) * 128], eye_b[:]), inc=(j2 == 7))
                    p.wait("vector", "T", p.cnt["T"])
                    V(lambda e, s, i=i, g4=g4, bank=bank: e.tensor_copy(
                        out=tern2T[:][:, g4 * 8:(g4 + 1) * 8, i * 128:(i + 1) * 128],
                        in_=bank[:].rearrange("p (a c) -> p a c", a=8)), inc=True)
                    bank_read(pk, "V", p.cnt["V"])
            LOAD(lambda e, s: e.dma_start(out=cwt[:], in_=cw.rearrange("(a p) c -> p a c", p=128)))
            LOAD(lambda e, s: e.dma_start(out=cbt[:], in_=cb.rearrange("o (a p) -> p (o a)", p=128), allow_slow_non_contiguous=True))

            p.barrier()
            stQ.close()
            ybuf = sb3("ybuf", [128, NCT, FC], F32)
            x2i = sb3("x2i", [128, NCT, FC], BF16)
            hbuf = sb3("hbuf", [128, FC + 2], F32)
            cvb = sb3("cvb", [128, FC], F32)
            mmx = sb3("mmx", [128, FC], F32)
            mmn = sb3("mmn", [128, FC], F32)
            s2b = sb3("s2b", [128, FC], F32)
            o2 = sb3("o2", [128, D], F32)
            yst = [sb3(f"yst{i}", [128, D], F32) for i in range(2)]

            for tc in range(NT):
                par = tc % 2
                for dep in lastread.get(("x1c", par), ()):
                    p.wait("sync", dep[0], dep[1])
                ld_x1 = LOAD(lambda e, s, tc=tc, par=par: e.dma_start(out=x1c[par][:], in_=x1scr[tc * 128:(tc + 1) * 128, :]))
                p.wait("vector", "LD", ld_x1)
                v_ln2 = ln_stats(par, (lambda lo, hi, par=par: x1c[par][:, lo:hi]), D, EPS_LN)
                p.wait("scalar", "V", v_ln2)
                SC(lambda e, s, par=par: e.activation(out=a2[:], in_=x1c[par][:], func=AF.Identity,
                                                      scale=stat[par][:, 19:20], bias=stat[par][:, 21:22]), inc=True)
                sc_a2 = p.cnt["SC"]
                lastread[("x1c", par)] = [("SC", sc_a2)]
                p.wait("vector", "SC", sc_a2)
                V(lambda e, s: e.tensor_reduce(out=qs[:, 2:3], in_=a2[:], axis=AX.X, op=AluOpType.max, apply_absolute_value=True))
                V(lambda e, s: e.drain())
                V(lambda e, s: e.tensor_scalar(out=qs[:, 1:2], in0=qs[:, 2:3], scalar1=1e-5, scalar2=None, op0=AluOpType.max))
                V(lambda e, s: e.drain())
                V(lambda e, s: e.reciprocal(out=qs[:, 3:4], in_=qs[:, 1:2]))
                V(lambda e, s: e.drain())
                V(lambda e, s: e.tensor_scalar(out=qs[:, 4:5], in0=qs[:, 3:4], scalar1=127.0, scalar2=None, op0=AluOpType.mult))
                V(lambda e, s: e.drain())
                V(lambda e, s, tc=tc: e.reciprocal(out=rxrow[:, tc:tc + 1], in_=qs[:, 4:5]))
                V(lambda e, s: e.tensor_scalar(out=xi16[:], in0=a2[:], scalar1=qs[:, 4:5], scalar2=None, op0=AluOpType.mult))
                V(lambda e, s: e.drain())
                V(lambda e, s: e.tensor_copy(out=xib[:], in_=xi16[:]), inc=True)
                v_xib = p.cnt["V"]
                p.wait("tensor", "V", v_xib)
                pk = ("pbf", par)
                bank = pbf[par]
                bank_guard(pk)
                for dk in range(8):
                    T(lambda e, s, dk=dk, bank=bank: e.transpose(bank[:, dk * 128:(dk + 1) * 128], xib[:, dk * 128:(dk + 1) * 128], eye_b[:]), inc=(dk == 7))
                p.wait("vector", "T", p.cnt["T"])
                V(lambda e, s, tc=tc, bank=bank: e.tensor_copy(
                    out=xintT[:][:, :, tc * 128:(tc + 1) * 128],
                    in_=bank[:].rearrange("p (a t) -> p a t", a=8)), inc=True)
                bank_read(pk, "V", p.cnt["V"])

            V(lambda e, s: e.drain())
            for tc in range(NT):
                V(lambda e, s, tc=tc: e.tensor_scalar(out=diag[:], in0=eye_f[:], scalar1=rxrow[:, tc:tc + 1], scalar2=meanw1b[:], op0=AluOpType.mult, op1=AluOpType.mult), inc=True)
                p.wait("tensor", "V", p.cnt["V"])
                bi = tc % 2
                bank_guard(bi)
                T(lambda e, s, bi=bi: e.matmul(bnk[bi][:, 0:128], ones128_f[:], diag[:], start=True, stop=True), inc=True)
                p.wait("vector", "T", p.cnt["T"])
                V(lambda e, s, tc=tc, bi=bi: e.tensor_copy(out=Rfull[:, tc * 128:(tc + 1) * 128], in_=bnk[bi][:, 0:128]), inc=True)
                bank_read(bi, "V", p.cnt["V"])

            if DEBUG:
                p.wait("gpsimd", "V", p.cnt["V"])
                STORE(lambda e, s: e.dma_start(out=d_t1, in_=tern1T[:]))
                STORE(lambda e, s: e.dma_start(out=d_t2, in_=tern2T[:]))
                STORE(lambda e, s: e.dma_start(out=d_xint, in_=xintT[:]))
                STORE(lambda e, s: e.dma_start(out=d_R, in_=Rfull[:]))
                STORE(lambda e, s: e.dma_start(out=d_mw, in_=qs[:]))
            p.barrier()

            y_st = [0, 0]
            for ci in range(NFC):
                t0 = ci * FC
                lo_pad = 1 if ci == 0 else 0
                hi_pad = 1 if ci == NFC - 1 else 0
                ncol = FC + 2 - lo_pad - hi_pad
                src0 = t0 - 1 + lo_pad
                for ct in range(NCT):
                    bi = ct % 2
                    bank_guard(bi)
                    for dk in range(8):
                        T(lambda e, s, ct=ct, dk=dk, bi=bi, src0=src0, ncol=ncol:
                          e.matmul(bnk[bi][:, 0:ncol],
                                   tern1T[:][:, dk, ct * 128:(ct + 1) * 128],
                                   xintT[:][:, dk, src0:src0 + ncol],
                                   start=(dk == 0), stop=(dk == 7)),
                          inc=(dk == 7))
                    t_m1 = p.cnt["T"]
                    p.wait("vector", "T", t_m1)
                    if lo_pad:
                        V(lambda e, s: e.memset(hbuf[:, 0:1], 0.0))
                        V(lambda e, s: e.drain())
                    if hi_pad:
                        V(lambda e, s: e.memset(hbuf[:, FC + 1:FC + 2], 0.0))
                        V(lambda e, s: e.drain())
                    V(lambda e, s, bi=bi, lo_pad=lo_pad, ncol=ncol, src0=src0:
                      e.tensor_tensor(out=hbuf[:, lo_pad:lo_pad + ncol], in0=bnk[bi][:, 0:ncol],
                                      in1=Rfull[:, src0:src0 + ncol], op=AluOpType.mult), inc=True)
                    bank_read(bi, "V", p.cnt["V"])
                    p.wait("vector", "SC", p.cnt["SC"])
                    V(lambda e, s, ct=ct: e.tensor_scalar(out=cvb[:], in0=hbuf[:, 0:FC], scalar1=cwt[:, ct, 0:1], scalar2=None, op0=AluOpType.mult))
                    V(lambda e, s, ct=ct: e.scalar_tensor_tensor(out=cvb[:], in0=hbuf[:, 1:FC + 1], scalar=cwt[:, ct, 1:2], in1=cvb[:], op0=AluOpType.mult, op1=AluOpType.add))
                    V(lambda e, s, ct=ct: e.scalar_tensor_tensor(out=cvb[:], in0=hbuf[:, 2:FC + 2], scalar=cwt[:, ct, 2:3], in1=cvb[:], op0=AluOpType.mult, op1=AluOpType.add), inc=True)
                    v_cv = p.cnt["V"]
                    if DEBUG and ci == 0 and ct == 0:
                        p.wait("gpsimd", "V", v_cv)
                        STORE(lambda e, s: e.dma_start(out=d_h, in_=hbuf[:]))
                        STORE(lambda e, s: e.dma_start(out=d_cv, in_=cvb[:]))
                        STORE(lambda e, s: e.dma_start(out=d_cw, in_=cwt[:]))
                        STORE(lambda e, s: e.dma_start(out=d_cb, in_=cbt[:]))
                        p.wait("vector", "G", p.cnt["G"])
                        p.wait("scalar", "G", p.cnt["G"])
                    p.wait("scalar", "V", v_cv)
                    SC(lambda e, s, ct=ct: e.activation(out=ybuf[:, ct, :], in_=cvb[:], func=AF.Silu, bias=cbt[:, ct:ct + 1]), inc=True)
                sc_y = p.cnt["SC"]
                if DEBUG and ci == 0:
                    p.wait("gpsimd", "SC", sc_y)
                    STORE(lambda e, s: e.dma_start(out=d_y0, in_=ybuf[:]))
                    p.wait("vector", "G", p.cnt["G"])
                p.wait("vector", "SC", sc_y)
                V(lambda e, s: e.tensor_copy(out=mmx[:], in_=ybuf[:, 0, :]))
                V(lambda e, s: e.tensor_copy(out=mmn[:], in_=ybuf[:, 0, :]), inc=True)
                for ct in range(1, NCT):
                    V(lambda e, s, ct=ct: e.tensor_tensor(out=mmx[:], in0=mmx[:], in1=ybuf[:, ct, :], op=AluOpType.max))
                    V(lambda e, s, ct=ct: e.tensor_tensor(out=mmn[:], in0=mmn[:], in1=ybuf[:, ct, :], op=AluOpType.min), inc=(ct == NCT - 1))
                V(lambda e, s: e.tensor_scalar(out=mmn[:], in0=mmn[:], scalar1=-1.0, scalar2=None, op0=AluOpType.mult))
                V(lambda e, s: e.tensor_tensor(out=mmx[:], in0=mmx[:], in1=mmn[:], op=AluOpType.max), inc=True)
                V(lambda e, s: e.drain())
                v_mx = p.cnt["V"]
                p.wait("tensor", "V", v_mx)
                for ts in range(2):
                    bank_guard(2 + ts)
                    T(lambda e, s, ts=ts: e.transpose(bnk[2 + ts][:, 0:128], mmx[:, ts * 128:(ts + 1) * 128], eye_f[:]), inc=True)
                t_tr = p.cnt["T"]
                p.wait("vector", "T", t_tr)
                for ts in range(2):
                    V(lambda e, s, ts=ts: e.tensor_reduce(out=qs[:, ts:ts + 1], in_=bnk[2 + ts][:, 0:128], axis=AX.X, op=AluOpType.max), inc=(ts == 1))
                bank_read(2, "V", p.cnt["V"])
                bank_read(3, "V", p.cnt["V"])
                V(lambda e, s: e.drain())
                V(lambda e, s: e.tensor_scalar(out=qs[:, 6:8], in0=qs[:, 0:2], scalar1=1e-5, scalar2=None, op0=AluOpType.max))
                V(lambda e, s: e.drain())
                V(lambda e, s: e.reciprocal(out=qs[:, 0:2], in_=qs[:, 6:8]))
                V(lambda e, s: e.drain())
                V(lambda e, s: e.tensor_scalar(out=qs[:, 2:4], in0=qs[:, 0:2], scalar1=127.0, scalar2=None, op0=AluOpType.mult))
                V(lambda e, s: e.drain())
                V(lambda e, s: e.reciprocal(out=qs[:, 4:6], in_=qs[:, 2:4]))
                V(lambda e, s: e.drain())
                V(lambda e, s: e.tensor_scalar(out=qs[:, 4:6], in0=qs[:, 4:6], scalar1=meanw2b[:], scalar2=None, op0=AluOpType.mult), inc=True)
                V(lambda e, s: e.drain())
                for ts in range(2):
                    V(lambda e, s, ts=ts: e.tensor_scalar(out=diag[:], in0=eye_f[:], scalar1=qs[:, 2 + ts:3 + ts], scalar2=None, op0=AluOpType.mult), inc=True)
                    p.wait("tensor", "V", p.cnt["V"])
                    bank_guard(4 + ts)
                    T(lambda e, s, ts=ts: e.matmul(bnk[4 + ts][:, 0:128], ones128_f[:], diag[:], start=True, stop=True), inc=True)
                    p.wait("vector", "T", p.cnt["T"])
                    V(lambda e, s, ts=ts: e.tensor_copy(out=s2b[:, ts * 128:(ts + 1) * 128], in_=bnk[4 + ts][:, 0:128]), inc=True)
                    bank_read(4 + ts, "V", p.cnt["V"])
                    V(lambda e, s: e.drain())
                for ct in range(NCT):
                    V(lambda e, s, ct=ct: e.tensor_tensor(out=xi16[:, 0:FC], in0=ybuf[:, ct, :], in1=s2b[:], op=AluOpType.mult))
                    V(lambda e, s: e.drain())
                    V(lambda e, s, ct=ct: e.tensor_copy(out=x2i[:, ct, :], in_=xi16[:, 0:FC]), inc=(ct == NCT - 1))
                v_x2 = p.cnt["V"]

                p.wait("tensor", "V", v_x2)
                for ts in range(2):
                    tg = (t0 + ts * 128) // 128
                    par = tg % 2
                    for dep in lastread.get(("x1c2", par), ()):
                        p.wait("sync", dep[0], dep[1])
                    ld_x1b = LOAD(lambda e, s, tg=tg, par=par: e.dma_start(out=x1c[par][:], in_=x1scr[tg * 128:(tg + 1) * 128, :]))
                    for dhf in range(2):
                        bi = 2 * ts + dhf
                        bank_guard(bi)
                        for ct in range(NCT):
                            T(lambda e, s, ct=ct, ts=ts, dhf=dhf, bi=bi:
                              e.matmul(bnk[bi][:, 0:512], x2i[:, ct, ts * 128:(ts + 1) * 128],
                                       tern2T[:, ct, dhf * 512:(dhf + 1) * 512],
                                       start=(ct == 0), stop=(ct == NCT - 1)),
                              inc=(ct == NCT - 1))
                    t_m2 = p.cnt["T"]
                    p.wait("vector", "T", t_m2)
                    for dhf in range(2):
                        V(lambda e, s, ts=ts, dhf=dhf, bi=2 * ts + dhf:
                          e.tensor_scalar(out=o2[:, dhf * 512:(dhf + 1) * 512], in0=bnk[bi][:, 0:512],
                                          scalar1=qs[:, 4 + ts:5 + ts], scalar2=None, op0=AluOpType.mult),
                          inc=(dhf == 1))
                    bank_read(2 * ts, "V", p.cnt["V"])
                    bank_read(2 * ts + 1, "V", p.cnt["V"])
                    if DEBUG and ci == 0 and ts == 0:
                        p.wait("gpsimd", "V", p.cnt["V"])
                        STORE(lambda e, s: e.dma_start(out=d_o2, in_=o2[:]))
                        p.wait("vector", "G", p.cnt["G"])
                    v_ln4 = ln_stats(par, (lambda lo, hi: o2[:, lo:hi]), D, EPS_LN)
                    p.wait("scalar", "V", v_ln4)
                    SC(lambda e, s, par=par: e.activation(out=a2[:], in_=o2[:], func=AF.Identity,
                                                          scale=stat[par][:, 19:20], bias=stat[par][:, 21:22]), inc=True)
                    sc_ln4 = p.cnt["SC"]
                    p.wait("vector", "SC", sc_ln4)
                    p.wait("vector", "LD", ld_x1b)
                    p.wait("vector", "G", y_st[par])
                    V(lambda e, s, par=par: e.tensor_tensor(out=yst[par][:], in0=a2[:], in1=x1c[par][:], op=AluOpType.add), inc=True)
                    lastread[("x1c2", par)] = [("V", p.cnt["V"])]
                    v_y = p.cnt["V"]
                    p.wait("gpsimd", "V", v_y)
                    y_st[par] = STORE(lambda e, s, tg=tg, par=par: e.dma_start(out=y_out[tg * 128:(tg + 1) * 128, :], in_=yst[par][:]))

            p.wait("gpsimd", "G", p.cnt["G"])

            block = st3.enter_context(nc.Block())
            p.realize(block, sems)
    return nc


_NC_CACHE = None
DEBUG = False
QK_PASSES = 3


def kernel(L0_attn_inputs, L0_attn_Wq, L0_attn_Wk, L0_attn_Wv, L0_attn_Wo,
           L0_attn_q_norm_w, L0_attn_k_norm_w, L0_ffn_w1, L0_ffn_conv_w,
           L0_ffn_conv_b, L0_ffn_w2):
    global _NC_CACHE
    x = np.ascontiguousarray(np.asarray(L0_attn_inputs, dtype=np.float32))
    B = x.shape[0]
    wq = np.ascontiguousarray(np.asarray(L0_attn_Wq, dtype=np.float32))
    wk = np.ascontiguousarray(np.asarray(L0_attn_Wk, dtype=np.float32))
    wv = np.ascontiguousarray(np.asarray(L0_attn_Wv, dtype=np.float32))
    wo = np.ascontiguousarray(np.asarray(L0_attn_Wo, dtype=np.float32))
    qn = np.ascontiguousarray(np.asarray(L0_attn_q_norm_w, dtype=np.float32)).reshape(1, DH)
    kn = np.ascontiguousarray(np.asarray(L0_attn_k_norm_w, dtype=np.float32)).reshape(1, DH)
    w1a = np.ascontiguousarray(np.asarray(L0_ffn_w1, dtype=np.float32))
    cwa = np.ascontiguousarray(np.asarray(L0_ffn_conv_w, dtype=np.float32)).reshape(DHID, 3)
    cba = np.ascontiguousarray(np.asarray(L0_ffn_conv_b, dtype=np.float32)).reshape(1, DHID)
    w2a = np.ascontiguousarray(np.asarray(L0_ffn_w2, dtype=np.float32))

    if _NC_CACHE is None:
        _NC_CACHE = build_nc()
    nc = _NC_CACHE
    in_maps = []
    for b in range(B):
        in_maps.append({"x": np.ascontiguousarray(x[b]), "Wq": wq, "Wk": wk,
                        "Wv": wv, "Wo": wo, "qnw": qn, "knw": kn, "w1": w1a,
                        "cw": cwa, "cb": cba, "w2": w2a})
    res = run_bass_kernel_spmd(nc, in_maps, core_ids=list(range(B)))
    out = np.stack([np.asarray(res.results[b]["y"]) for b in range(B)], axis=0)
    return out.astype(np.float32)


# version 08 (final)
# speedup vs baseline: 1.0226x; 1.0226x over previous
import math
import numpy as np
import concourse.bass as bass
import concourse.mybir as mybir
from concourse.bass_utils import run_bass_kernel_spmd
from concourse.alu_op_type import AluOpType
from contextlib import ExitStack

F32 = mybir.dt.float32
BF16 = mybir.dt.bfloat16
F8 = mybir.dt.float8e4
I16 = mybir.dt.int16
I32 = mybir.dt.int32
AF = mybir.ActivationFunctionType
AX = mybir.AxisListType

S = 2048
D = 1024
H, KV, DH = 16, 4, 64
DHID = 4096
NT = S // 128
EPS_RMS = 1e-6
EPS_LN = 1e-5
ROPE_BASE = 10000.0
TQ = 512
NQTR = S // TQ
FC = 256
NFC = S // FC
NCT = DHID // 128

ENGS = ("sync", "tensor", "vector", "scalar", "gpsimd")
TWO_PI = 2.0 * math.pi
RC1 = float(np.float32(TWO_PI))
RC2 = float(np.float32(TWO_PI - RC1))
RC3 = float(np.float32(TWO_PI - RC1 - RC2))


class Prog:

    def __init__(self):
        self.items = {e: [] for e in ENGS}
        self.cnt = {s: 0 for s in ("LD", "G", "T", "V", "SC")}

    def op(self, eng, fn, inc=None):
        if inc is not None:
            self.cnt[inc[0]] += inc[1]
        self.items[eng].append((fn, inc))
        return self.cnt[inc[0]] if inc is not None else None

    def wait(self, eng, semname, val):
        if val <= 0:
            return
        assert val <= self.cnt[semname], (eng, semname, val, self.cnt[semname])
        self.items[eng].append(
            (lambda e, s, _sn=semname, _v=val: e.wait_ge(s[_sn], _v), None))

    def selfwait(self, eng, sem):
        fn, inc = self.items[eng][-1]
        if inc is None:
            self.cnt[sem] += 1
            self.items[eng][-1] = (fn, (sem, 1))
        self.wait(eng, sem, self.cnt[sem])

    def barrier(self):
        snap = dict(self.cnt)
        prod = {"sync": "LD", "gpsimd": "G", "tensor": "T", "vector": "V",
                "scalar": "SC"}
        for eng in ENGS:
            for sem, v in snap.items():
                if prod[eng] != sem:
                    self.wait(eng, sem, v)

    def realize(self, block, sems):
        def mk(eng):
            def body(e):
                for fn, inc in self.items[eng]:
                    r = fn(e, sems)
                    if inc is not None:
                        r.then_inc(sems[inc[0]], inc[1])
            return body
        for eng in ENGS:
            getattr(block, eng)(mk(eng))


def build_nc():
    nc = bass.Bass("TRN2", target_bir_lowering=False)

    x_in = nc.dram_tensor("x", [S, D], F32, kind="ExternalInput").ap()
    Wq = nc.dram_tensor("Wq", [D, D], F32, kind="ExternalInput").ap()
    Wk = nc.dram_tensor("Wk", [KV * DH, D], F32, kind="ExternalInput").ap()
    Wv = nc.dram_tensor("Wv", [KV * DH, D], F32, kind="ExternalInput").ap()
    Wo = nc.dram_tensor("Wo", [D, D], F32, kind="ExternalInput").ap()
    qnw = nc.dram_tensor("qnw", [1, DH], F32, kind="ExternalInput").ap()
    knw = nc.dram_tensor("knw", [1, DH], F32, kind="ExternalInput").ap()
    w1 = nc.dram_tensor("w1", [DHID, D], F32, kind="ExternalInput").ap()
    cw = nc.dram_tensor("cw", [DHID, 3], F32, kind="ExternalInput").ap()
    cb = nc.dram_tensor("cb", [1, DHID], F32, kind="ExternalInput").ap()
    w2 = nc.dram_tensor("w2", [D, DHID], F32, kind="ExternalInput").ap()
    y_out = nc.dram_tensor("y", [S, D], F32, kind="ExternalOutput").ap()
    _kind = {"kind": "ExternalOutput"} if DEBUG else {}
    if DEBUG:
        d_khi = nc.dram_tensor("d_khi", [128, KV, S], BF16, kind="ExternalOutput").ap()
        d_klo = nc.dram_tensor("d_klo", [128, KV, S], BF16, kind="ExternalOutput").ap()
        d_vext = nc.dram_tensor("d_vext", [128, NT * KV * (DH + 1)], F32, kind="ExternalOutput").ap()
        d_E = nc.dram_tensor("d_E", [128, TQ], F32, kind="ExternalOutput").ap()
        d_av = nc.dram_tensor("d_av", [128, 8, TQ], F32, kind="ExternalOutput").ap()
        d_S = nc.dram_tensor("d_S", [128, TQ], F32, kind="ExternalOutput").ap()
        d_qq = nc.dram_tensor("d_qq", [128, 8, TQ], BF16, kind="ExternalOutput").ap()
        d_qlo2 = nc.dram_tensor("d_qlo2", [128, 8, TQ], BF16, kind="ExternalOutput").ap()
        d_cos = nc.dram_tensor("d_cos", [128, NT, DH], F32, kind="ExternalOutput").ap()
        d_sin = nc.dram_tensor("d_sin", [128, NT, DH], F32, kind="ExternalOutput").ap()
        d_pos = nc.dram_tensor("d_pos", [128, NT], F32, kind="ExternalOutput").ap()
        d_t1 = nc.dram_tensor("d_t1", [128, 8, DHID], F8, kind="ExternalOutput").ap()
        d_t2 = nc.dram_tensor("d_t2", [128, NCT, D], F8, kind="ExternalOutput").ap()
        d_xint = nc.dram_tensor("d_xint", [128, 8, S], BF16, kind="ExternalOutput").ap()
        d_R = nc.dram_tensor("d_R", [128, S], F32, kind="ExternalOutput").ap()
        d_mw = nc.dram_tensor("d_mw", [128, 8], F32, kind="ExternalOutput").ap()
        d_y0 = nc.dram_tensor("d_y0", [128, NCT, FC], F32, kind="ExternalOutput").ap()
        d_o2 = nc.dram_tensor("d_o2", [128, D], F32, kind="ExternalOutput").ap()
        d_h = nc.dram_tensor("d_h", [128, FC + 2], F32, kind="ExternalOutput").ap()
        d_cv = nc.dram_tensor("d_cv", [128, FC], F32, kind="ExternalOutput").ap()
        d_cw = nc.dram_tensor("d_cw", [128, NCT, 3], F32, kind="ExternalOutput").ap()
        d_cb = nc.dram_tensor("d_cb", [128, NCT], F32, kind="ExternalOutput").ap()
        d_klo2 = nc.dram_tensor("d_klo2", [128, KV, S], BF16, kind="ExternalOutput").ap()
        d_E0 = nc.dram_tensor("d_E0", [128, TQ], F32, kind="ExternalOutput").ap()
    x1scr = nc.dram_tensor("x1scr", [S, D], F32, **_kind).ap()
    qscr_hi = nc.dram_tensor("qscr_hi", [128, 8, S], BF16, **_kind).ap()
    qscr_lo = nc.dram_tensor("qscr_lo", [128, 8, S], BF16, **_kind).ap()
    woscr = nc.dram_tensor("woscr", [128, 8, D], F32, **_kind).ap()

    p = Prog()

    def T(fn, inc=False):
        return p.op("tensor", fn, ("T", 1) if inc else None)

    def V(fn, inc=False):
        return p.op("vector", fn, ("V", 1) if inc else None)

    def SC(fn, inc=False):
        return p.op("scalar", fn, ("SC", 1) if inc else None)

    def LOAD(fn):
        return p.op("sync", fn, ("LD", 16))

    def STORE(fn):
        return p.op("gpsimd", fn, ("G", 16))

    def GOP(fn):
        return p.op("gpsimd", fn, ("G", 1))

    with ExitStack() as st:
        sb = lambda n, shp, dt: st.enter_context(nc.sbuf_tensor(n, shp, dt))
        ps = lambda n, shp, dt: st.enter_context(nc.psum_tensor(n, shp, dt))

        eye_f = sb("eye_f", [128, 128], F32)
        eye_b = sb("eye_b", [128, 128], BF16)
        ones1_64 = sb("ones1_64", [1, 64], F32)
        ones1_128 = sb("ones1_128", [1, 128], F32)
        ones128_f = sb("ones128_f", [128, 128], F32)
        io32 = sb("io32", [128, 128], I32)
        rxrow = sb("rxrow", [128, NT], F32)
        meanw1b = sb("meanw1b", [128, 1], F32)
        meanw2b = sb("meanw2b", [128, 1], F32)
        xch = [sb(f"xch{i}", [128, D], F32) for i in range(2)]
        c_epsln = sb("c_epsln", [128, 1], F32)
        c_epsrms = sb("c_epsrms", [128, 1], F32)
        stat = [sb(f"stat{i}", [128, 24], F32) for i in range(2)]
        sdump = sb("sdump", [128, 512], F32)

        bnk = [ps(f"b{i}", [128, 512], F32) for i in range(6)]
        pbf = [ps("pbf1", [128, 1024], BF16), ps("pbf2", [128, 1024], BF16)]
        lastread = {}

        def bank_guard(key):
            for dep in lastread.get(key, ()):
                p.wait("tensor", dep[0], dep[1])

        def bank_read(key, sem, vv):
            lastread[key] = [(sem, vv)]

        def bank_read_add(key, sem, vv):
            lastread.setdefault(key, []).append((sem, vv))

        sems = {}
        for nm in ("LD", "G", "T", "V", "SC"):
            sems[nm] = st.enter_context(nc.semaphore(nm))

        def ln_stats(sidx, src_fn, n_free, eps):
            stt = stat[sidx]
            half = n_free // 2
            V(lambda e, s, stt=stt: e.tensor_reduce(out=stt[:, 4:5], in_=src_fn(0, half), axis=AX.X, op=AluOpType.add))
            V(lambda e, s, stt=stt: e.tensor_reduce(out=stt[:, 5:6], in_=src_fn(half, n_free), axis=AX.X, op=AluOpType.add))
            p.selfwait("vector", "V")
            V(lambda e, s, stt=stt: e.tensor_tensor(out=stt[:, 0:1], in0=stt[:, 4:5], in1=stt[:, 5:6], op=AluOpType.add), inc=True)
            p.wait("scalar", "V", p.cnt["V"])
            SC(lambda e, s, stt=stt: e.activation(out=sdump[:, 0:half], in_=src_fn(0, half), func=AF.Square, accum_out=stt[:, 6:7]))
            SC(lambda e, s, stt=stt: e.activation(out=sdump[:, 0:half], in_=src_fn(half, n_free), func=AF.Square, accum_out=stt[:, 7:8]), inc=True)
            p.wait("vector", "SC", p.cnt["SC"])
            V(lambda e, s, stt=stt: e.tensor_tensor(out=stt[:, 1:2], in0=stt[:, 6:7], in1=stt[:, 7:8], op=AluOpType.add))
            p.selfwait("vector", "V")
            V(lambda e, s, stt=stt: e.tensor_scalar(out=stt[:, 12:13], in0=stt[:, 0:1], scalar1=1.0 / n_free, scalar2=None, op0=AluOpType.mult))
            p.selfwait("vector", "V")
            V(lambda e, s, stt=stt: e.tensor_tensor(out=stt[:, 3:4], in0=stt[:, 12:13], in1=stt[:, 12:13], op=AluOpType.mult))
            V(lambda e, s, stt=stt: e.tensor_scalar(out=stt[:, 2:3], in0=stt[:, 1:2], scalar1=1.0 / n_free, scalar2=None, op0=AluOpType.mult))
            p.selfwait("vector", "V")
            V(lambda e, s, stt=stt: e.tensor_tensor(out=stt[:, 13:14], in0=stt[:, 2:3], in1=stt[:, 3:4], op=AluOpType.subtract), inc=True)
            p.wait("scalar", "V", p.cnt["V"])
            SC(lambda e, s, stt=stt: e.activation(out=stt[:, 15:16], in_=stt[:, 13:14], func=AF.Sqrt, bias=c_epsln[:]), inc=True)
            p.wait("vector", "SC", p.cnt["SC"])
            V(lambda e, s, stt=stt: e.reciprocal(out=stt[:, 19:20], in_=stt[:, 15:16]))
            p.selfwait("vector", "V")
            V(lambda e, s, stt=stt: e.tensor_tensor(out=stt[:, 20:21], in0=stt[:, 12:13], in1=stt[:, 19:20], op=AluOpType.mult))
            p.selfwait("vector", "V")
            V(lambda e, s, stt=stt: e.tensor_scalar(out=stt[:, 21:22], in0=stt[:, 20:21], scalar1=-1.0, scalar2=None, op0=AluOpType.mult), inc=True)
            p.selfwait("vector", "V")
            return p.cnt["V"]

        stA = st.enter_context(ExitStack())
        sbA = lambda n, shp, dt: stA.enter_context(nc.sbuf_tensor(n, shp, dt))
        kT_hi = sbA("kT_hi", [128, KV, S], BF16)
        kT_lo = sbA("kT_lo", [128, KV, S], BF16)
        v_ext = sbA("v_ext", [128, NT, KV, DH + 1], F32)
        with ExitStack() as st0:
            sb0 = lambda n, shp, dt: st0.enter_context(nc.sbuf_tensor(n, shp, dt))
            WT_hi = sb0("WT_hi", [128, 8, 1536], BF16)
            WT_lo = sb0("WT_lo", [128, 8, 1536], BF16)
            nrm = sb0("nrm", [1, 2 * DH], F32)
            invrep = sb0("invrep", [128, DH], F32)
            wq_rep = sb0("wq_rep", [128, DH], F32)
            wq_rot = sb0("wq_rot", [128, DH], F32)
            wk_rep = sb0("wk_rep", [128, DH], F32)
            wk_rot = sb0("wk_rot", [128, DH], F32)
            pos = sb0("pos", [128, NT], F32)
            ang = sb0("ang", [128, DH], F32)
            ang2 = sb0("ang2", [128, DH], F32)
            kf = sb0("kf", [128, DH], F32)
            k32s = sb0("k32s", [128, DH], I32)
            tsc = sb0("tsc", [128, DH], F32)
            trig = sb0("trig", [128, DH], F32)
            cosq = sb0("cosq", [128, NT, DH], F32)
            sinq = sb0("sinq", [128, NT, DH], F32)
            cosk = sb0("cosk", [128, NT, DH], F32)
            sink = sb0("sink", [128, NT, DH], F32)

            LOAD(lambda e, s: e.dma_start(out=nrm[0:1, 0:DH], in_=qnw))
            ld_nrm = LOAD(lambda e, s: e.dma_start(out=nrm[0:1, DH:2 * DH], in_=knw))

            for fn_ in (AF.Exp, AF.Sin, AF.Sqrt, AF.Square, AF.Silu, AF.Identity, AF.Sigmoid):
                SC(lambda e, s, fn_=fn_: e.activation(out=sdump[:, 0:1], in_=sdump[:, 0:1], func=fn_))
            SC(lambda e, s: e.activation(out=sdump[:, 0:1], in_=sdump[:, 1:2], func=AF.Exp), inc=True)

            GOP(lambda e, s: e.iota(io32[:, 0:NT], pattern=[[128, NT]], base=-1, channel_multiplier=1))
            g_io = p.cnt["G"]
            p.wait("vector", "G", g_io)
            V(lambda e, s: e.tensor_copy(out=pos[:], in_=io32[:, 0:NT]))
            p.selfwait("vector", "V")
            V(lambda e, s: e.tensor_scalar(out=pos[:], in0=pos[:], scalar1=0.0, scalar2=None, op0=AluOpType.max), inc=True)
            p.selfwait("vector", "V")
            v_pos = p.cnt["V"]
            p.wait("gpsimd", "V", v_pos)
            GOP(lambda e, s: e.iota(io32[:], pattern=[[1, 128]], base=0, channel_multiplier=-1))
            g_io2 = p.cnt["G"]
            GOP(lambda e, s: e.iota(io32[0:1, 0:32], pattern=[[1, 32]], base=0, channel_multiplier=0))
            g_io3 = p.cnt["G"]
            p.wait("vector", "G", g_io2)
            p.wait("gpsimd", "V", v_pos)
            p.wait("vector", "G", g_io3)
            V(lambda e, s: e.tensor_copy(out=ang[0:1, 0:32], in_=io32[0:1, 0:32]))
            p.selfwait("vector", "V")
            V(lambda e, s: e.tensor_copy(out=ang[0:1, 32:64], in_=ang[0:1, 0:32]), inc=True)
            v_invf = p.cnt["V"]
            p.wait("gpsimd", "V", v_invf)
            GOP(lambda e, s: e.iota(io32[:], pattern=[[1, 128]], base=0, channel_multiplier=-1))
            g_io4 = p.cnt["G"]
            p.wait("vector", "G", g_io4)
            V(lambda e, s: e.tensor_single_scalar(out=eye_f[:], in_=io32[:], scalar=0.0, op=AluOpType.is_equal))
            p.selfwait("vector", "V")
            V(lambda e, s: e.tensor_copy(out=eye_b[:], in_=eye_f[:]))
            V(lambda e, s: e.memset(ones1_64[:], 1.0))
            V(lambda e, s: e.memset(c_epsln[:], EPS_LN))
            V(lambda e, s: e.memset(c_epsrms[:], EPS_RMS))
            V(lambda e, s: e.memset(ones1_128[:], 1.0))
            V(lambda e, s: e.memset(ones128_f[:], 1.0))
            V(lambda e, s: e.memset(v_ext[:][:, :, :, DH:DH + 1], 1.0), inc=True)
            v_const = p.cnt["V"]
            p.wait("scalar", "V", v_const)
            SC(lambda e, s: e.activation(out=ang2[0:1, :], in_=ang[0:1, :], func=AF.Exp,
                                         scale=-math.log(ROPE_BASE) / 32.0), inc=True)
            sc_invf = p.cnt["SC"]

            p.wait("tensor", "SC", sc_invf)
            p.wait("tensor", "LD", ld_nrm)
            T(lambda e, s: e.matmul(bnk[0][0:128, 0:DH], ones1_128[:], ang2[0:1, :], start=True, stop=True), inc=True)
            T(lambda e, s: e.matmul(bnk[1][0:128, 0:2 * DH], ones1_128[:], nrm[0:1, :], start=True, stop=True), inc=True)
            t_bc = p.cnt["T"]
            p.wait("vector", "T", t_bc)
            V(lambda e, s: e.tensor_copy(out=invrep[:], in_=bnk[0][0:128, 0:DH]))
            V(lambda e, s: e.tensor_copy(out=wq_rep[:], in_=bnk[1][0:128, 0:DH]))
            V(lambda e, s: e.tensor_copy(out=wk_rep[:], in_=bnk[1][0:128, DH:2 * DH]))
            V(lambda e, s: e.tensor_copy(out=wq_rot[:, 0:32], in_=wq_rep[:, 32:64]))
            V(lambda e, s: e.tensor_copy(out=wq_rot[:, 32:64], in_=wq_rep[:, 0:32]))
            V(lambda e, s: e.tensor_copy(out=wk_rot[:, 0:32], in_=wk_rep[:, 32:64]))
            V(lambda e, s: e.tensor_copy(out=wk_rot[:, 32:64], in_=wk_rep[:, 0:32]), inc=True)
            p.selfwait("vector", "V")
            v_bc = p.cnt["V"]
            bank_read(0, "V", v_bc)
            bank_read(1, "V", v_bc)

            def range_reduce(src_ap):
                V(lambda e, s: e.tensor_scalar(out=k32s[:], in0=src_ap, scalar1=1.0 / TWO_PI, scalar2=None, op0=AluOpType.mult))
                p.selfwait("vector", "V")
                V(lambda e, s: e.tensor_copy(out=kf[:], in_=k32s[:]))
                p.selfwait("vector", "V")
                V(lambda e, s: e.tensor_scalar(out=tsc[:], in0=kf[:], scalar1=RC1, scalar2=None, op0=AluOpType.mult))
                p.selfwait("vector", "V")
                V(lambda e, s: e.tensor_tensor(out=trig[:], in0=src_ap, in1=tsc[:], op=AluOpType.subtract))
                V(lambda e, s: e.tensor_scalar(out=tsc[:], in0=kf[:], scalar1=RC2, scalar2=None, op0=AluOpType.mult))
                p.selfwait("vector", "V")
                V(lambda e, s: e.tensor_tensor(out=trig[:], in0=trig[:], in1=tsc[:], op=AluOpType.subtract))
                V(lambda e, s: e.tensor_scalar(out=tsc[:], in0=kf[:], scalar1=RC3, scalar2=None, op0=AluOpType.mult))
                p.selfwait("vector", "V")
                V(lambda e, s: e.tensor_tensor(out=trig[:], in0=trig[:], in1=tsc[:], op=AluOpType.subtract), inc=True)

            for tc in range(NT):
                V(lambda e, s, tc=tc: e.tensor_scalar(out=ang[:], in0=invrep[:], scalar1=pos[:, tc:tc + 1], scalar2=None, op0=AluOpType.mult))
                p.selfwait("vector", "V")
                V(lambda e, s: e.tensor_scalar(out=ang2[:], in0=ang[:], scalar1=math.pi / 2.0, scalar2=None, op0=AluOpType.add), inc=True)
                range_reduce(ang[:])
                p.wait("scalar", "V", p.cnt["V"])
                sc_s = SC(lambda e, s: e.activation(out=tsc[:], in_=trig[:], func=AF.Sin), inc=True)
                p.wait("vector", "SC", sc_s)
                V(lambda e, s, tc=tc: e.tensor_tensor(out=sinq[:, tc, :], in0=tsc[:], in1=wq_rot[:], op=AluOpType.mult))
                V(lambda e, s, tc=tc: e.tensor_tensor(out=sink[:, tc, :], in0=tsc[:], in1=wk_rot[:], op=AluOpType.mult), inc=True)
                range_reduce(ang2[:])
                p.wait("scalar", "V", p.cnt["V"])
                sc_c = SC(lambda e, s: e.activation(out=tsc[:], in_=trig[:], func=AF.Sin), inc=True)
                p.wait("vector", "SC", sc_c)
                V(lambda e, s, tc=tc: e.tensor_tensor(out=cosq[:, tc, :], in0=tsc[:], in1=wq_rep[:], op=AluOpType.mult))
                V(lambda e, s, tc=tc: e.tensor_tensor(out=cosk[:, tc, :], in0=tsc[:], in1=wk_rep[:], op=AluOpType.mult), inc=True)

            with ExitStack() as stW1:
                wostg = stW1.enter_context(nc.sbuf_tensor("wostg", [128, 8, D], F32))
                wot = stW1.enter_context(nc.sbuf_tensor("wot", [128, D], F32))
                ld_wo = LOAD(lambda e, s: e.dma_start(out=wostg[:], in_=Wo.rearrange("(a p) d -> p a d", p=128)))
                p.wait("tensor", "LD", ld_wo)
                g_wot = 0
                for m in range(8):
                    for dk in range(8):
                        bi = dk % 2
                        bank_guard(bi)
                        T(lambda e, s, m=m, dk=dk, bi=bi:
                          e.transpose(bnk[bi][:, 0:128], wostg[:, dk, m * 128:(m + 1) * 128], eye_f[:]), inc=True)
                        p.wait("vector", "T", p.cnt["T"])
                        p.wait("vector", "G", g_wot)
                        V(lambda e, s, dk=dk, bi=bi:
                          e.tensor_copy(out=wot[:, dk * 128:(dk + 1) * 128], in_=bnk[bi][:, 0:128]), inc=True)
                        bank_read(bi, "V", p.cnt["V"])
                    v_wot = p.cnt["V"]
                    p.wait("gpsimd", "V", v_wot)
                    g_wot = STORE(lambda e, s, m=m: e.dma_start(out=woscr[:, m, :], in_=wot[:]))
                t_wo_done = p.cnt["T"]
                g_wo_done = p.cnt["G"]

            stW2 = st0.enter_context(ExitStack())
            wstg = stW2.enter_context(nc.sbuf_tensor("wstg", [128, 12, D], F32))
            p.wait("sync", "T", t_wo_done)
            p.wait("sync", "G", g_wo_done)
            LOAD(lambda e, s: e.dma_start(out=wstg[:, 0:8, :], in_=Wq.rearrange("(a p) d -> p a d", p=128)))
            LOAD(lambda e, s: e.dma_start(out=wstg[:, 8:10, :], in_=Wk.rearrange("(a p) d -> p a d", p=128)))
            ld_wqkv = LOAD(lambda e, s: e.dma_start(out=wstg[:, 10:12, :], in_=Wv.rearrange("(a p) d -> p a d", p=128)))

            if DEBUG:
                p.wait("gpsimd", "V", p.cnt["V"])
                STORE(lambda e, s: e.dma_start(out=d_cos, in_=cosq[:]))
                STORE(lambda e, s: e.dma_start(out=d_sin, in_=sinq[:]))
                STORE(lambda e, s: e.dma_start(out=d_pos, in_=pos[:]))
            p.wait("tensor", "LD", ld_wqkv)
            for dk in range(8):
                for jj in (0, 4, 8):
                    bi = (dk + jj // 4) % 2
                    bank_guard(bi)
                    for j2 in range(4):
                        T(lambda e, s, j=jj + j2, dk=dk, bi=bi, j2=j2:
                          e.transpose(bnk[bi][:, j2 * 128:(j2 + 1) * 128], wstg[:, j, dk * 128:(dk + 1) * 128], eye_f[:]),
                          inc=(j2 == 3))
                    p.wait("vector", "T", p.cnt["T"])
                    V(lambda e, s, dk=dk, jj=jj, bi=bi:
                      e.tensor_copy(out=WT_hi[:, dk, jj * 128:(jj + 4) * 128], in_=bnk[bi][:, 0:512]))
                    p.selfwait("vector", "V")
                    V(lambda e, s, dk=dk, jj=jj, bi=bi:
                      e.tensor_tensor(out=WT_lo[:, dk, jj * 128:(jj + 4) * 128], in0=bnk[bi][:, 0:512],
                                      in1=WT_hi[:, dk, jj * 128:(jj + 4) * 128], op=AluOpType.subtract), inc=True)
                    bank_read(bi, "V", p.cnt["V"])
            stW2.close()
            a_hi = [sb0(f"a_hi{i}", [128, D], BF16) for i in range(2)]
            a_lo = [sb0(f"a_lo{i}", [128, D], BF16) for i in range(2)]
            a_f32 = [sb0(f"a_f32{i}", [128, D], F32) for i in range(2)]
            aT_hi = [sb0(f"aT_hi{i}", [128, 8, 128], BF16) for i in range(2)]
            aT_lo = [sb0(f"aT_lo{i}", [128, 8, 128], BF16) for i in range(2)]
            qa = sb0("qa", [128, 1280], F32)
            qb = sb0("qb", [128, 1280], F32)
            qc = sb0("qc", [128, 1280], F32)
            qd = sb0("qd", [128, 1280], F32)
            qsp_hi = [sb0(f"qsp_hi{i}", [128, 1280], BF16) for i in range(2)]
            qsp_lo = [sb0(f"qsp_lo{i}", [128, 1280], BF16) for i in range(2)]
            qTs = [sb0(f"qTs{i}", [128, 8, 128], BF16) for i in range(4)]
            ss20 = sb0("ss20", [128, 20], F32)
            u20 = sb0("u20", [128, 20], F32)
            r20 = sb0("r20", [128, 20], F32)
            t20 = sb0("t20", [128, 20], F32)

            ev_done = [0, 0]
            ax_done = [0, 0]
            qts_st = [0, 0, 0, 0]
            v_qsp_cur = [0, 0]
            qk_tp_done = [0, 0]

            for tc in range(NT):
                par = tc % 2
                bs = bnk[0:3] if par == 0 else bnk[3:6]
                bkeys = (0, 1, 2) if par == 0 else (3, 4, 5)

                if tc >= 2:
                    p.wait("sync", "V", ax_done[par])
                ld_x = LOAD(lambda e, s, tc=tc, par=par: e.dma_start(
                    out=xch[par][:], in_=x_in[tc * 128:(tc + 1) * 128, :]))

                p.wait("vector", "LD", ld_x)
                v_ln = ln_stats(par, (lambda lo, hi, par=par: xch[par][:, lo:hi]), D, EPS_LN)
                p.wait("scalar", "V", v_ln)
                SC(lambda e, s, par=par: e.activation(out=a_hi[par][:], in_=xch[par][:], func=AF.Identity,
                                                      scale=stat[par][:, 19:20], bias=stat[par][:, 21:22]))
                SC(lambda e, s, par=par: e.activation(out=a_f32[par][:], in_=xch[par][:], func=AF.Identity,
                                                      scale=stat[par][:, 19:20], bias=stat[par][:, 21:22]), inc=True)
                sc_app = p.cnt["SC"]
                p.wait("vector", "SC", sc_app)
                V(lambda e, s, par=par: e.tensor_tensor(out=a_lo[par][:], in0=a_f32[par][:], in1=a_hi[par][:], op=AluOpType.subtract), inc=True)
                v_asplit = p.cnt["V"]

                p.wait("tensor", "V", v_asplit)
                for src, dst in ((a_hi, aT_hi), (a_lo, aT_lo)):
                    pk = ("pbf", par)
                    bank = pbf[par]
                    bank_guard(pk)
                    for dk in range(8):
                        T(lambda e, s, src=src, par=par, dk=dk, bank=bank:
                          e.transpose(bank[:, dk * 128:(dk + 1) * 128], src[par][:, dk * 128:(dk + 1) * 128], eye_b[:]),
                          inc=(dk == 7))
                    p.wait("vector", "T", p.cnt["T"])
                    V(lambda e, s, dst=dst, par=par, bank=bank:
                      e.tensor_copy(out=dst[par][:], in_=bank[:].rearrange("p (a b) -> p a b", a=8)), inc=True)
                    bank_read(pk, "V", p.cnt["V"])
                v_aT = p.cnt["V"]

                for bi in bkeys:
                    bank_guard(bi)
                p.wait("tensor", "V", v_aT)
                first = [True, True, True]
                for pi, (lsrc, rsrc) in enumerate(((aT_hi, WT_hi), (aT_hi, WT_lo), (aT_lo, WT_hi))):
                    for dk in range(8):
                        for ns in range(3):
                            lastmm = (pi == 2 and dk == 7 and ns == 2)
                            T(lambda e, s, lsrc=lsrc, rsrc=rsrc, par=par, dk=dk, ns=ns, st_=first[ns], bs=bs:
                              e.matmul(bs[ns][:, 0:512], lsrc[par][:, dk, :], rsrc[:, dk, ns * 512:(ns + 1) * 512],
                                       start=st_, stop=True),
                              inc=lastmm)
                            first[ns] = False
                t_qkv = p.cnt["T"]

                p.wait("scalar", "T", t_qkv)
                for hseg in range(20):
                    col = hseg * 64
                    SC(lambda e, s, bs=bs, bi=col // 512, off=col % 512, hseg=hseg:
                       e.activation(out=sdump[:, 0:64], in_=bs[bi][:, off:off + 64], func=AF.Square,
                                    accum_out=ss20[:, hseg:hseg + 1]), inc=(hseg == 19))
                sc_ss = p.cnt["SC"]
                p.wait("vector", "SC", sc_ss)
                V(lambda e, s: e.tensor_scalar(out=u20[:], in0=ss20[:], scalar1=1.0 / 64.0, scalar2=None, op0=AluOpType.mult), inc=True)
                p.wait("scalar", "V", p.cnt["V"])
                SC(lambda e, s: e.activation(out=t20[:], in_=u20[:], func=AF.Sqrt, bias=c_epsrms[:]), inc=True)
                p.wait("vector", "SC", p.cnt["SC"])
                V(lambda e, s: e.reciprocal(out=r20[:], in_=t20[:]))
                p.selfwait("vector", "V")
                for hseg in range(20):
                    col = hseg * 64
                    V(lambda e, s, bs=bs, bi=col // 512, off=col % 512, hseg=hseg, col=col:
                      e.tensor_scalar(out=qa[:, col:col + 64], in0=bs[bi][:, off:off + 64],
                                      scalar1=r20[:, hseg:hseg + 1], scalar2=None, op0=AluOpType.mult))
                V(lambda e, s, bs=bs, tc=tc:
                  e.tensor_copy(out=v_ext[:][:, tc, :, 0:DH],
                                in_=bs[2][:, 256:512].rearrange("p (b c) -> p b c", b=KV)), inc=True)
                ev_done[par] = p.cnt["V"]
                for bi in bkeys:
                    bank_read(bi, "V", ev_done[par])
                    bank_read_add(bi, "SC", sc_ss)

                for hh in range(16):
                    V(lambda e, s, hh=hh, tc=tc: e.tensor_copy(out=qd[:, hh * 64:(hh + 1) * 64], in_=cosq[:, tc, :]))
                for kk2 in range(4):
                    V(lambda e, s, kk2=kk2, tc=tc: e.tensor_copy(out=qd[:, 1024 + kk2 * 64:1024 + (kk2 + 1) * 64], in_=cosk[:, tc, :]))
                p.selfwait("vector", "V")
                V(lambda e, s: e.tensor_tensor(out=qb[:], in0=qa[:], in1=qd[:], op=AluOpType.mult))
                V(lambda e, s: e.tensor_scalar(
                    out=qc[:].rearrange("p (a b) -> p a b", a=20)[:, :, 0:32],
                    in0=qa[:].rearrange("p (a b) -> p a b", a=20)[:, :, 32:64],
                    scalar1=-1.0, scalar2=None, op0=AluOpType.mult))
                V(lambda e, s: e.tensor_copy(
                    out=qc[:].rearrange("p (a b) -> p a b", a=20)[:, :, 32:64],
                    in_=qa[:].rearrange("p (a b) -> p a b", a=20)[:, :, 0:32]))
                p.selfwait("vector", "V")
                for hh in range(16):
                    V(lambda e, s, hh=hh, tc=tc: e.tensor_copy(out=qd[:, hh * 64:(hh + 1) * 64], in_=sinq[:, tc, :]))
                for kk2 in range(4):
                    V(lambda e, s, kk2=kk2, tc=tc: e.tensor_copy(out=qd[:, 1024 + kk2 * 64:1024 + (kk2 + 1) * 64], in_=sink[:, tc, :]))
                p.selfwait("vector", "V")
                V(lambda e, s: e.tensor_tensor(out=qa[:], in0=qc[:], in1=qd[:], op=AluOpType.mult))
                V(lambda e, s: e.tensor_tensor(out=qc[:], in0=qb[:], in1=qa[:], op=AluOpType.add))
                p.wait("vector", "T", qk_tp_done[par])
                V(lambda e, s, par=par: e.tensor_copy(out=qsp_hi[par][:], in_=qc[:]))
                p.selfwait("vector", "V")
                V(lambda e, s, par=par: e.tensor_tensor(out=qsp_lo[par][:], in0=qc[:], in1=qsp_hi[par][:], op=AluOpType.subtract), inc=True)
                ax_done[par] = p.cnt["V"]
                v_qsp_cur[par] = p.cnt["V"]

                def emit_qk_transposes(tcx):
                    parx = tcx % 2
                    p.wait("tensor", "V", v_qsp_cur[parx])
                    for si, (src, qdst, kdst, scr_t) in enumerate(
                            ((qsp_hi[parx], qTs[0 + 2 * parx], kT_hi, qscr_hi),
                             (qsp_lo[parx], qTs[1 + 2 * parx], kT_lo, qscr_lo))):
                        pk = ("pbf", parx)
                        bank = pbf[parx]
                        bank_guard(pk)
                        for dk in range(8):
                            T(lambda e, s, src=src, dk=dk, bank=bank:
                              e.transpose(bank[:, dk * 128:(dk + 1) * 128], src[:, dk * 128:(dk + 1) * 128], eye_b[:]),
                              inc=(dk == 7))
                        p.wait("vector", "T", p.cnt["T"])
                        p.wait("vector", "G", qts_st[si + 2 * parx])
                        V(lambda e, s, qdst=qdst, bank=bank: e.tensor_copy(out=qdst[:], in_=bank[:].rearrange("p (a b) -> p a b", a=8)), inc=True)
                        bank_read(pk, "V", p.cnt["V"])
                        v_qT = p.cnt["V"]
                        p.wait("gpsimd", "V", v_qT)
                        qts_st[si + 2 * parx] = STORE(lambda e, s, qdst=qdst, scr_t=scr_t, tcx=tcx:
                                                      e.dma_start(out=scr_t[:, :, tcx * 128:(tcx + 1) * 128], in_=qdst[:]))
                        bank_guard(pk)
                        for j in range(2):
                            T(lambda e, s, src=src, j=j, bank=bank:
                              e.transpose(bank[:, j * 128:(j + 1) * 128], src[:, 1024 + j * 128:1024 + (j + 1) * 128], eye_b[:]),
                              inc=(j == 1))
                        p.wait("vector", "T", p.cnt["T"])
                        for j in range(2):
                            for half in range(2):
                                kv = 2 * j + half
                                V(lambda e, s, kdst=kdst, kv=kv, tcx=tcx, bank=bank, j=j, half=half:
                                  e.tensor_copy(out=kdst[0:64, kv, tcx * 128:(tcx + 1) * 128],
                                                in_=bank[half * 64:half * 64 + 64, j * 128:(j + 1) * 128]))
                                V(lambda e, s, kdst=kdst, kv=kv, tcx=tcx, bank=bank, j=j, half=half:
                                  e.tensor_copy(out=kdst[64:128, kv, tcx * 128:(tcx + 1) * 128],
                                                in_=bank[half * 64:half * 64 + 64, j * 128:(j + 1) * 128]),
                                  inc=(j == 1 and half == 1))
                        bank_read(pk, "V", p.cnt["V"])
                    qk_tp_done[parx] = p.cnt["T"]

                if tc >= 1:
                    emit_qk_transposes(tc - 1)
            emit_qk_transposes(NT - 1)

            if DEBUG:
                p.wait("gpsimd", "V", p.cnt["V"])
                STORE(lambda e, s: e.dma_start(out=d_khi, in_=kT_hi[:]))
                STORE(lambda e, s: e.dma_start(out=d_klo, in_=kT_lo[:]))
                STORE(lambda e, s: e.dma_start(out=d_vext, in_=v_ext[:].rearrange("p a b c -> p (a b c)")))
            p.barrier()

        lastread.clear()
        with ExitStack() as st2:
            sb2 = lambda n, shp, dt: st2.enter_context(nc.sbuf_tensor(n, shp, dt))
            Ebuf = sb2("Ebuf", [128, NT, TQ], F32)
            av_sb = sb2("av_sb", [128, 8, TQ], F32)
            qq_hi = [sb2(f"qq_hi{i}", [128, 8, TQ], BF16) for i in range(2)]
            qq_lo = [sb2(f"qq_lo{i}", [128, 8, TQ], BF16) for i in range(2)]
            wo_all = sb2("wo_all", [128, 8, D], F32)
            Zr = sb2("Zr", [1, TQ], F32)
            rZb = sb2("rZb", [64, TQ], F32)
            ln3 = sb2("ln3", [128, D], F32)
            yx1 = [sb2(f"yx1{i}", [128, D], F32) for i in range(2)]

            LOAD(lambda e, s: e.dma_start(out=wo_all[:], in_=woscr))
            ld_qq = {}
            for qtr in range(2):
                ld_qq[(qtr, 0)] = LOAD(lambda e, s, qtr=qtr: e.dma_start(out=qq_hi[qtr][:], in_=qscr_hi[:, :, qtr * TQ:(qtr + 1) * TQ]))
                ld_qq[(qtr, 1)] = LOAD(lambda e, s, qtr=qtr: e.dma_start(out=qq_lo[qtr][:], in_=qscr_lo[:, :, qtr * TQ:(qtr + 1) * TQ]))

            if DEBUG:
                p.wait("gpsimd", "LD", ld_qq[(0, 1)])
                STORE(lambda e, s: e.dma_start(out=d_qq, in_=qq_hi[0][:]))
            x1_st = [0, 0]
            for qtr in range(NQTR):
                qpar = qtr % 2
                qhi, qlo = qq_hi[qpar], qq_lo[qpar]
                p.wait("tensor", "LD", ld_qq[(qtr, 1)])
                for h in range(H):
                    kv = h // 4
                    hb = (h % 2) * 64
                    for c in range(NT):
                        bi = c % 2
                        bank_guard(bi)
                        _passes = ((kT_hi, qhi), (kT_hi, qlo), (kT_lo, qhi))[:QK_PASSES]
                        for pi, (lsrc, rsrc) in enumerate(_passes):
                            T(lambda e, s, lsrc=lsrc, rsrc=rsrc, hb=hb, kv=kv, c=c, h=h, bi=bi,
                                     st_=(pi == 0), sp_=(pi == len(_passes) - 1):
                              e.matmul(bnk[bi][:, 0:TQ],
                                       lsrc[hb:hb + 64, kv, c * 128:(c + 1) * 128],
                                       rsrc[hb:hb + 64, h // 2, :],
                                       start=st_, stop=sp_),
                              inc=(pi == len(_passes) - 1))
                        t_qk = p.cnt["T"]
                        p.wait("scalar", "T", t_qk)
                        sc_e = SC(lambda e, s, c=c, bi=bi:
                                  e.activation(out=Ebuf[:, c, :], in_=bnk[bi][:, 0:TQ], func=AF.Exp), inc=True)
                        if DEBUG and qtr == 0 and h == 0 and c == 0:
                            p.wait("vector", "T", t_qk)
                            V(lambda e, s, bi=bi: e.tensor_copy(out=ln3[:, 0:TQ], in_=bnk[bi][:, 0:TQ]), inc=True)
                            p.wait("gpsimd", "V", p.cnt["V"])
                            STORE(lambda e, s: e.dma_start(out=d_S, in_=ln3[:, 0:TQ]))
                            p.wait("gpsimd", "SC", sc_e)
                            STORE(lambda e, s: e.dma_start(out=d_E0, in_=Ebuf[:, 0, :]))
                            STORE(lambda e, s: e.dma_start(out=d_qlo2, in_=qq_lo[0][:]))
                            STORE(lambda e, s: e.dma_start(out=d_klo2, in_=kT_lo[:]))
                            p.wait("tensor", "G", p.cnt["G"])
                            sc_e = p.cnt["SC"]
                            bank_read(bi, "V", p.cnt["V"])
                        bank_read(bi, "SC", sc_e)
                    sc_exp_end = p.cnt["SC"]
                    bank_guard(2)
                    for c in range(NT):
                        p.wait("tensor", "SC", sc_exp_end - (NT - 1 - c))
                        T(lambda e, s, c=c, kv=kv:
                          e.matmul(bnk[2][0:65, 0:TQ], v_ext[:][:, c, kv, :], Ebuf[:, c, :],
                                   start=(c == 0), stop=(c == NT - 1)),
                          inc=(c == NT - 1))
                    t_av = p.cnt["T"]
                    p.wait("vector", "T", t_av)
                    V(lambda e, s: e.tensor_copy(out=Zr[0:1, :], in_=bnk[2][64:65, 0:TQ]), inc=True)
                    v_zr = p.cnt["V"]
                    p.wait("tensor", "V", v_zr)
                    bank_guard(3)
                    T(lambda e, s: e.matmul(bnk[3][0:64, 0:TQ], ones1_64[:], Zr[0:1, :], start=True, stop=True), inc=True)
                    t_zb = p.cnt["T"]
                    p.wait("vector", "T", t_zb)
                    V(lambda e, s: e.reciprocal(out=rZb[:], in_=bnk[3][0:64, 0:TQ]), inc=True)
                    bank_read(3, "V", p.cnt["V"])
                    V(lambda e, s, hb=hb, h=h: e.tensor_tensor(out=av_sb[hb:hb + 64, h // 2, :], in0=bnk[2][0:64, 0:TQ], in1=rZb[:], op=AluOpType.mult), inc=True)
                    bank_read(2, "V", p.cnt["V"])
                v_avn = p.cnt["V"]
                if DEBUG and qtr == 0:
                    p.wait("gpsimd", "V", v_avn)
                    STORE(lambda e, s: e.dma_start(out=d_av, in_=av_sb[:]))
                    STORE(lambda e, s: e.dma_start(out=d_E, in_=Ebuf[:, 0, :]))

                p.wait("tensor", "V", v_avn)
                for ts in range(TQ // 128):
                    tg = qtr * (TQ // 128) + ts
                    par = tg % 2
                    for dep in lastread.get(("yx1", par), ()):
                        p.wait("sync", dep[0], dep[1])
                    ld_x2 = LOAD(lambda e, s, tg=tg, par=par: e.dma_start(out=xch[par][:], in_=x_in[tg * 128:(tg + 1) * 128, :]))
                    for dhf in range(2):
                        bank_guard(4 + dhf)
                        for m in range(8):
                            T(lambda e, s, m=m, ts=ts, dhf=dhf:
                              e.matmul(bnk[4 + dhf][:, 0:512], av_sb[:, m, ts * 128:(ts + 1) * 128],
                                       wo_all[:, m, dhf * 512:(dhf + 1) * 512],
                                       start=(m == 0), stop=(m == 7)),
                              inc=(m == 7))
                    t_wo = p.cnt["T"]
                    p.wait("vector", "T", t_wo)
                    v_ln3 = ln_stats(par, (lambda lo, hi: bnk[4][:, lo:hi] if hi <= 512 else bnk[5][:, lo - 512:hi - 512]), D, EPS_LN)
                    p.wait("scalar", "V", v_ln3)
                    for dhf in range(2):
                        SC(lambda e, s, dhf=dhf, par=par:
                           e.activation(out=ln3[:, dhf * 512:(dhf + 1) * 512], in_=bnk[4 + dhf][:, 0:512], func=AF.Identity,
                                        scale=stat[par][:, 19:20], bias=stat[par][:, 21:22]), inc=(dhf == 1))
                    sc_ln3 = p.cnt["SC"]
                    bank_read(4, "SC", sc_ln3)
                    bank_read(5, "SC", sc_ln3)
                    p.wait("vector", "SC", sc_ln3)
                    p.wait("vector", "LD", ld_x2)
                    p.wait("vector", "G", x1_st[par])
                    V(lambda e, s, par=par: e.tensor_tensor(out=yx1[par][:], in0=ln3[:], in1=xch[par][:], op=AluOpType.add), inc=True)
                    lastread[("yx1", par)] = [("V", p.cnt["V"])]
                    v_x1 = p.cnt["V"]
                    p.wait("gpsimd", "V", v_x1)
                    x1_st[par] = STORE(lambda e, s, tg=tg, par=par: e.dma_start(out=x1scr[tg * 128:(tg + 1) * 128, :], in_=yx1[par][:]))
                if qtr + 2 < NQTR:
                    p.wait("sync", "T", p.cnt["T"])
                    ld_qq[(qtr + 2, 0)] = LOAD(lambda e, s, q2=qtr + 2, qpar=qpar: e.dma_start(out=qq_hi[qpar][:], in_=qscr_hi[:, :, q2 * TQ:(q2 + 1) * TQ]))
                    ld_qq[(qtr + 2, 1)] = LOAD(lambda e, s, q2=qtr + 2, qpar=qpar: e.dma_start(out=qq_lo[qpar][:], in_=qscr_lo[:, :, q2 * TQ:(q2 + 1) * TQ]))

            p.barrier()

        stA.close()
        lastread.clear()
        with ExitStack() as st3:
            sb3 = lambda n, shp, dt: st3.enter_context(nc.sbuf_tensor(n, shp, dt))
            tern1T = sb3("tern1T", [128, 8, DHID], F8)
            tern2T = sb3("tern2T", [128, NCT, D], F8)
            xintT = sb3("xintT", [128, 8, S], BF16)
            Rfull = sb3("Rfull", [128, S], F32)
            cwt = sb3("cwt", [128, NCT, 3], F32)
            cbt = sb3("cbt", [128, NCT], F32)
            x1c = [sb3(f"x1c{i}", [128, D], F32) for i in range(2)]
            a2 = sb3("a2", [128, D], F32)
            xi16 = sb3("xi16", [128, D], I16)
            xib = sb3("xib", [128, D], BF16)
            qs = sb3("qs", [128, 8], F32)
            diag = sb3("diag", [128, 128], F32)
            stQ = st3.enter_context(ExitStack())
            sbQ = lambda n, shp, dt: stQ.enter_context(nc.sbuf_tensor(n, shp, dt))
            wst = [sbQ(f"wst{i}", [128, DHID], F32) for i in range(2)]
            wsum = sbQ("wsum", [128, 40], F32)
            wi16 = sbQ("wi16", [128, DHID], I16)
            wtb = sbQ("wtb", [128, DHID], BF16)

            for i in range(32):
                par = i % 2
                for dep in lastread.get(("wst", par), ()):
                    p.wait("sync", dep[0], dep[1])
                ld_w1 = LOAD(lambda e, s, i=i, par=par: e.dma_start(out=wst[par][:, 0:D], in_=w1[i * 128:(i + 1) * 128, :]))
                p.wait("vector", "LD", ld_w1)
                V(lambda e, s, i=i, par=par: e.tensor_reduce(out=wsum[:, i:i + 1], in_=wst[par][:, 0:D], axis=AX.X, op=AluOpType.add, apply_absolute_value=True), inc=True)
                lastread[("wst", par)] = [("V", p.cnt["V"])]
            for i in range(8):
                par = i % 2
                for dep in lastread.get(("wst", par), ()):
                    p.wait("sync", dep[0], dep[1])
                ld_w2 = LOAD(lambda e, s, i=i, par=par: e.dma_start(out=wst[par][:], in_=w2[i * 128:(i + 1) * 128, :]))
                p.wait("vector", "LD", ld_w2)
                V(lambda e, s, i=i, par=par: e.tensor_reduce(out=wsum[:, 32 + i:33 + i], in_=wst[par][:], axis=AX.X, op=AluOpType.add, apply_absolute_value=True), inc=True)
                lastread[("wst", par)] = [("V", p.cnt["V"])]
            p.selfwait("vector", "V")
            V(lambda e, s: e.tensor_reduce(out=qs[:, 0:1], in_=wsum[:, 0:32], axis=AX.X, op=AluOpType.add))
            V(lambda e, s: e.tensor_reduce(out=qs[:, 1:2], in_=wsum[:, 32:40], axis=AX.X, op=AluOpType.add), inc=True)
            p.selfwait("vector", "V")
            p.wait("tensor", "V", p.cnt["V"])
            bank_guard(0)
            T(lambda e, s: e.matmul(bnk[0][0:1, 0:2], ones128_f[:, 0:1], qs[:, 0:2], start=True, stop=True), inc=True)
            p.wait("vector", "T", p.cnt["T"])
            V(lambda e, s: e.tensor_copy(out=qs[0:1, 2:4], in_=bnk[0][0:1, 0:2]), inc=True)
            bank_read(0, "V", p.cnt["V"])
            p.selfwait("vector", "V")
            V(lambda e, s: e.tensor_scalar(out=qs[0:1, 4:6], in0=qs[0:1, 2:4], scalar1=1.0 / (DHID * D), scalar2=1e-5, op0=AluOpType.mult, op1=AluOpType.max), inc=True)
            p.selfwait("vector", "V")
            v_mean = p.cnt["V"]
            p.wait("tensor", "V", v_mean)
            bank_guard(1)
            T(lambda e, s: e.matmul(bnk[1][0:128, 0:2], ones1_128[:], qs[0:1, 4:6], start=True, stop=True), inc=True)
            p.wait("vector", "T", p.cnt["T"])
            V(lambda e, s: e.tensor_copy(out=meanw1b[:], in_=bnk[1][0:128, 0:1]))
            V(lambda e, s: e.tensor_copy(out=meanw2b[:], in_=bnk[1][0:128, 1:2]), inc=True)
            bank_read(1, "V", p.cnt["V"])
            p.selfwait("vector", "V")
            V(lambda e, s: e.reciprocal(out=qs[:, 6:7], in_=meanw1b[:]))
            V(lambda e, s: e.reciprocal(out=qs[:, 7:8], in_=meanw2b[:]), inc=True)
            p.selfwait("vector", "V")

            for i in range(32):
                par = i % 2
                for dep in lastread.get(("wst", par), ()):
                    p.wait("sync", dep[0], dep[1])
                ld_w1 = LOAD(lambda e, s, i=i, par=par: e.dma_start(out=wst[par][:, 0:D], in_=w1[i * 128:(i + 1) * 128, :]))
                p.wait("vector", "LD", ld_w1)
                V(lambda e, s, par=par: e.tensor_scalar(out=wi16[:, 0:D], in0=wst[par][:, 0:D], scalar1=qs[:, 6:7], scalar2=None, op0=AluOpType.mult))
                lastread[("wst", par)] = [("V", p.cnt["V"] + 1)]
                V(lambda e, s: e.tensor_scalar(out=wtb[:, 0:D], in0=wi16[:, 0:D], scalar1=-1.0, scalar2=1.0, op0=AluOpType.max, op1=AluOpType.min), inc=True)
                p.wait("tensor", "V", p.cnt["V"])
                pk = ("pbf", i % 2)
                bank = pbf[i % 2]
                bank_guard(pk)
                for dk in range(8):
                    T(lambda e, s, dk=dk, bank=bank: e.transpose(bank[:, dk * 128:(dk + 1) * 128], wtb[:, dk * 128:(dk + 1) * 128], eye_b[:]), inc=(dk == 7))
                p.wait("vector", "T", p.cnt["T"])
                V(lambda e, s, i=i, bank=bank: e.tensor_copy(
                    out=tern1T[:][:, :, i * 128:(i + 1) * 128],
                    in_=bank[:].rearrange("p (a c) -> p a c", a=8)), inc=True)
                bank_read(pk, "V", p.cnt["V"])
            for i in range(8):
                par = i % 2
                for dep in lastread.get(("wst", par), ()):
                    p.wait("sync", dep[0], dep[1])
                ld_w2 = LOAD(lambda e, s, i=i, par=par: e.dma_start(out=wst[par][:], in_=w2[i * 128:(i + 1) * 128, :]))
                p.wait("vector", "LD", ld_w2)
                V(lambda e, s, par=par: e.tensor_scalar(out=wi16[:], in0=wst[par][:], scalar1=qs[:, 7:8], scalar2=None, op0=AluOpType.mult))
                lastread[("wst", par)] = [("V", p.cnt["V"] + 1)]
                V(lambda e, s: e.tensor_scalar(out=wtb[:], in0=wi16[:], scalar1=-1.0, scalar2=1.0, op0=AluOpType.max, op1=AluOpType.min), inc=True)
                p.wait("tensor", "V", p.cnt["V"])
                for g4 in range(4):
                    pk = ("pbf", g4 % 2)
                    bank = pbf[g4 % 2]
                    bank_guard(pk)
                    for j2 in range(8):
                        ct = g4 * 8 + j2
                        T(lambda e, s, ct=ct, bank=bank, j2=j2: e.transpose(bank[:, j2 * 128:(j2 + 1) * 128], wtb[:, ct * 128:(ct + 1) * 128], eye_b[:]), inc=(j2 == 7))
                    p.wait("vector", "T", p.cnt["T"])
                    V(lambda e, s, i=i, g4=g4, bank=bank: e.tensor_copy(
                        out=tern2T[:][:, g4 * 8:(g4 + 1) * 8, i * 128:(i + 1) * 128],
                        in_=bank[:].rearrange("p (a c) -> p a c", a=8)), inc=True)
                    bank_read(pk, "V", p.cnt["V"])
            LOAD(lambda e, s: e.dma_start(out=cwt[:], in_=cw.rearrange("(a p) c -> p a c", p=128)))
            LOAD(lambda e, s: e.dma_start(out=cbt[:], in_=cb.rearrange("o (a p) -> p (o a)", p=128), allow_slow_non_contiguous=True))

            p.barrier()
            stQ.close()
            ybuf = sb3("ybuf", [128, NCT, FC], F32)
            x2i = sb3("x2i", [128, NCT, FC], BF16)
            hbuf = sb3("hbuf", [128, FC + 2], F32)
            cvb = sb3("cvb", [128, FC], F32)
            mmx = sb3("mmx", [128, FC], F32)
            mmn = sb3("mmn", [128, FC], F32)
            s2b = sb3("s2b", [128, FC], F32)
            o2 = sb3("o2", [128, D], F32)
            yst = [sb3(f"yst{i}", [128, D], F32) for i in range(2)]

            for tc in range(NT):
                par = tc % 2
                for dep in lastread.get(("x1c", par), ()):
                    p.wait("sync", dep[0], dep[1])
                ld_x1 = LOAD(lambda e, s, tc=tc, par=par: e.dma_start(out=x1c[par][:], in_=x1scr[tc * 128:(tc + 1) * 128, :]))
                p.wait("vector", "LD", ld_x1)
                v_ln2 = ln_stats(par, (lambda lo, hi, par=par: x1c[par][:, lo:hi]), D, EPS_LN)
                p.wait("scalar", "V", v_ln2)
                SC(lambda e, s, par=par: e.activation(out=a2[:], in_=x1c[par][:], func=AF.Identity,
                                                      scale=stat[par][:, 19:20], bias=stat[par][:, 21:22]), inc=True)
                sc_a2 = p.cnt["SC"]
                lastread[("x1c", par)] = [("SC", sc_a2)]
                p.wait("vector", "SC", sc_a2)
                V(lambda e, s: e.tensor_reduce(out=qs[:, 2:3], in_=a2[:], axis=AX.X, op=AluOpType.max, apply_absolute_value=True))
                p.selfwait("vector", "V")
                V(lambda e, s: e.tensor_scalar(out=qs[:, 1:2], in0=qs[:, 2:3], scalar1=1e-5, scalar2=None, op0=AluOpType.max))
                p.selfwait("vector", "V")
                V(lambda e, s: e.reciprocal(out=qs[:, 3:4], in_=qs[:, 1:2]))
                p.selfwait("vector", "V")
                V(lambda e, s: e.tensor_scalar(out=qs[:, 4:5], in0=qs[:, 3:4], scalar1=127.0, scalar2=None, op0=AluOpType.mult))
                p.selfwait("vector", "V")
                V(lambda e, s, tc=tc: e.reciprocal(out=rxrow[:, tc:tc + 1], in_=qs[:, 4:5]))
                V(lambda e, s: e.tensor_scalar(out=xi16[:], in0=a2[:], scalar1=qs[:, 4:5], scalar2=None, op0=AluOpType.mult))
                p.selfwait("vector", "V")
                V(lambda e, s: e.tensor_copy(out=xib[:], in_=xi16[:]), inc=True)
                v_xib = p.cnt["V"]
                p.wait("tensor", "V", v_xib)
                pk = ("pbf", par)
                bank = pbf[par]
                bank_guard(pk)
                for dk in range(8):
                    T(lambda e, s, dk=dk, bank=bank: e.transpose(bank[:, dk * 128:(dk + 1) * 128], xib[:, dk * 128:(dk + 1) * 128], eye_b[:]), inc=(dk == 7))
                p.wait("vector", "T", p.cnt["T"])
                V(lambda e, s, tc=tc, bank=bank: e.tensor_copy(
                    out=xintT[:][:, :, tc * 128:(tc + 1) * 128],
                    in_=bank[:].rearrange("p (a t) -> p a t", a=8)), inc=True)
                bank_read(pk, "V", p.cnt["V"])

            p.selfwait("vector", "V")
            for tc in range(NT):
                V(lambda e, s, tc=tc: e.tensor_scalar(out=diag[:], in0=eye_f[:], scalar1=rxrow[:, tc:tc + 1], scalar2=meanw1b[:], op0=AluOpType.mult, op1=AluOpType.mult), inc=True)
                p.wait("tensor", "V", p.cnt["V"])
                bi = tc % 2
                bank_guard(bi)
                T(lambda e, s, bi=bi: e.matmul(bnk[bi][:, 0:128], ones128_f[:], diag[:], start=True, stop=True), inc=True)
                p.wait("vector", "T", p.cnt["T"])
                V(lambda e, s, tc=tc, bi=bi: e.tensor_copy(out=Rfull[:, tc * 128:(tc + 1) * 128], in_=bnk[bi][:, 0:128]), inc=True)
                bank_read(bi, "V", p.cnt["V"])

            if DEBUG:
                p.wait("gpsimd", "V", p.cnt["V"])
                STORE(lambda e, s: e.dma_start(out=d_t1, in_=tern1T[:]))
                STORE(lambda e, s: e.dma_start(out=d_t2, in_=tern2T[:]))
                STORE(lambda e, s: e.dma_start(out=d_xint, in_=xintT[:]))
                STORE(lambda e, s: e.dma_start(out=d_R, in_=Rfull[:]))
                STORE(lambda e, s: e.dma_start(out=d_mw, in_=qs[:]))
            p.barrier()

            y_st = [0, 0]
            for ci in range(NFC):
                t0 = ci * FC
                lo_pad = 1 if ci == 0 else 0
                hi_pad = 1 if ci == NFC - 1 else 0
                ncol = FC + 2 - lo_pad - hi_pad
                src0 = t0 - 1 + lo_pad
                for ct in range(NCT):
                    bi = ct % 2
                    bank_guard(bi)
                    for dk in range(8):
                        T(lambda e, s, ct=ct, dk=dk, bi=bi, src0=src0, ncol=ncol:
                          e.matmul(bnk[bi][:, 0:ncol],
                                   tern1T[:][:, dk, ct * 128:(ct + 1) * 128],
                                   xintT[:][:, dk, src0:src0 + ncol],
                                   start=(dk == 0), stop=(dk == 7)),
                          inc=(dk == 7))
                    t_m1 = p.cnt["T"]
                    p.wait("vector", "T", t_m1)
                    if lo_pad:
                        V(lambda e, s: e.memset(hbuf[:, 0:1], 0.0))
                        p.selfwait("vector", "V")
                    if hi_pad:
                        V(lambda e, s: e.memset(hbuf[:, FC + 1:FC + 2], 0.0))
                        p.selfwait("vector", "V")
                    V(lambda e, s, bi=bi, lo_pad=lo_pad, ncol=ncol, src0=src0:
                      e.tensor_tensor(out=hbuf[:, lo_pad:lo_pad + ncol], in0=bnk[bi][:, 0:ncol],
                                      in1=Rfull[:, src0:src0 + ncol], op=AluOpType.mult), inc=True)
                    bank_read(bi, "V", p.cnt["V"])
                    p.wait("vector", "SC", p.cnt["SC"])
                    V(lambda e, s, ct=ct: e.tensor_scalar(out=cvb[:], in0=hbuf[:, 0:FC], scalar1=cwt[:, ct, 0:1], scalar2=None, op0=AluOpType.mult))
                    V(lambda e, s, ct=ct: e.scalar_tensor_tensor(out=cvb[:], in0=hbuf[:, 1:FC + 1], scalar=cwt[:, ct, 1:2], in1=cvb[:], op0=AluOpType.mult, op1=AluOpType.add))
                    V(lambda e, s, ct=ct: e.scalar_tensor_tensor(out=cvb[:], in0=hbuf[:, 2:FC + 2], scalar=cwt[:, ct, 2:3], in1=cvb[:], op0=AluOpType.mult, op1=AluOpType.add), inc=True)
                    v_cv = p.cnt["V"]
                    if DEBUG and ci == 0 and ct == 0:
                        p.wait("gpsimd", "V", v_cv)
                        STORE(lambda e, s: e.dma_start(out=d_h, in_=hbuf[:]))
                        STORE(lambda e, s: e.dma_start(out=d_cv, in_=cvb[:]))
                        STORE(lambda e, s: e.dma_start(out=d_cw, in_=cwt[:]))
                        STORE(lambda e, s: e.dma_start(out=d_cb, in_=cbt[:]))
                        p.wait("vector", "G", p.cnt["G"])
                        p.wait("scalar", "G", p.cnt["G"])
                    p.wait("scalar", "V", v_cv)
                    SC(lambda e, s, ct=ct: e.activation(out=ybuf[:, ct, :], in_=cvb[:], func=AF.Silu, bias=cbt[:, ct:ct + 1]), inc=True)
                sc_y = p.cnt["SC"]
                if DEBUG and ci == 0:
                    p.wait("gpsimd", "SC", sc_y)
                    STORE(lambda e, s: e.dma_start(out=d_y0, in_=ybuf[:]))
                    p.wait("vector", "G", p.cnt["G"])
                p.wait("tensor", "SC", sc_y)
                for ts in range(2):
                    for g in range(8):
                        bx = 2 + ((g + 2 * ts) % 2)
                        bank_guard(bx)
                        for j in range(4):
                            T(lambda e, s, ct=g * 4 + j, ts=ts, j=j, bx=bx:
                              e.transpose(bnk[bx][:, j * 128:(j + 1) * 128], ybuf[:, ct, ts * 128:(ts + 1) * 128], eye_f[:]),
                              inc=(j == 3))
                        p.wait("vector", "T", p.cnt["T"])
                        V(lambda e, s, ts=ts, g=g, bx=bx: e.tensor_reduce(out=mmx[:, ts * 8 + g:ts * 8 + g + 1], in_=bnk[bx][:, 0:512], axis=AX.X, op=AluOpType.max, apply_absolute_value=True), inc=True)
                        bank_read(bx, "V", p.cnt["V"])
                p.selfwait("vector", "V")
                for ts in range(2):
                    V(lambda e, s, ts=ts: e.tensor_reduce(out=qs[:, ts:ts + 1], in_=mmx[:, ts * 8:ts * 8 + 8], axis=AX.X, op=AluOpType.max), inc=(ts == 1))
                p.selfwait("vector", "V")
                V(lambda e, s: e.tensor_scalar(out=qs[:, 6:8], in0=qs[:, 0:2], scalar1=1e-5, scalar2=None, op0=AluOpType.max))
                p.selfwait("vector", "V")
                V(lambda e, s: e.reciprocal(out=qs[:, 0:2], in_=qs[:, 6:8]))
                p.selfwait("vector", "V")
                V(lambda e, s: e.tensor_scalar(out=qs[:, 2:4], in0=qs[:, 0:2], scalar1=127.0, scalar2=None, op0=AluOpType.mult))
                p.selfwait("vector", "V")
                V(lambda e, s: e.reciprocal(out=qs[:, 4:6], in_=qs[:, 2:4]))
                p.selfwait("vector", "V")
                V(lambda e, s: e.tensor_scalar(out=qs[:, 4:6], in0=qs[:, 4:6], scalar1=meanw2b[:], scalar2=None, op0=AluOpType.mult), inc=True)
                p.selfwait("vector", "V")
                for ts in range(2):
                    V(lambda e, s, ts=ts: e.tensor_scalar(out=diag[:], in0=eye_f[:], scalar1=qs[:, 2 + ts:3 + ts], scalar2=None, op0=AluOpType.mult), inc=True)
                    p.wait("tensor", "V", p.cnt["V"])
                    bank_guard(4 + ts)
                    T(lambda e, s, ts=ts: e.matmul(bnk[4 + ts][:, 0:128], ones128_f[:], diag[:], start=True, stop=True), inc=True)
                    p.wait("vector", "T", p.cnt["T"])
                    V(lambda e, s, ts=ts: e.tensor_copy(out=s2b[:, ts * 128:(ts + 1) * 128], in_=bnk[4 + ts][:, 0:128]), inc=True)
                    bank_read(4 + ts, "V", p.cnt["V"])
                    p.selfwait("vector", "V")
                for ct in range(NCT):
                    V(lambda e, s, ct=ct: e.tensor_tensor(out=xi16[:, 0:FC], in0=ybuf[:, ct, :], in1=s2b[:], op=AluOpType.mult))
                    p.selfwait("vector", "V")
                    V(lambda e, s, ct=ct: e.tensor_copy(out=x2i[:, ct, :], in_=xi16[:, 0:FC]), inc=(ct == NCT - 1))
                v_x2 = p.cnt["V"]

                p.wait("tensor", "V", v_x2)
                for ts in range(2):
                    tg = (t0 + ts * 128) // 128
                    par = tg % 2
                    for dep in lastread.get(("x1c2", par), ()):
                        p.wait("sync", dep[0], dep[1])
                    ld_x1b = LOAD(lambda e, s, tg=tg, par=par: e.dma_start(out=x1c[par][:], in_=x1scr[tg * 128:(tg + 1) * 128, :]))
                    for dhf in range(2):
                        bi = 2 * ts + dhf
                        bank_guard(bi)
                        for ct in range(NCT):
                            T(lambda e, s, ct=ct, ts=ts, dhf=dhf, bi=bi:
                              e.matmul(bnk[bi][:, 0:512], x2i[:, ct, ts * 128:(ts + 1) * 128],
                                       tern2T[:, ct, dhf * 512:(dhf + 1) * 512],
                                       start=(ct == 0), stop=(ct == NCT - 1)),
                              inc=(ct == NCT - 1))
                    t_m2 = p.cnt["T"]
                    p.wait("vector", "T", t_m2)
                    for dhf in range(2):
                        V(lambda e, s, ts=ts, dhf=dhf, bi=2 * ts + dhf:
                          e.tensor_scalar(out=o2[:, dhf * 512:(dhf + 1) * 512], in0=bnk[bi][:, 0:512],
                                          scalar1=qs[:, 4 + ts:5 + ts], scalar2=None, op0=AluOpType.mult),
                          inc=(dhf == 1))
                    bank_read(2 * ts, "V", p.cnt["V"])
                    bank_read(2 * ts + 1, "V", p.cnt["V"])
                    if DEBUG and ci == 0 and ts == 0:
                        p.wait("gpsimd", "V", p.cnt["V"])
                        STORE(lambda e, s: e.dma_start(out=d_o2, in_=o2[:]))
                        p.wait("vector", "G", p.cnt["G"])
                    v_ln4 = ln_stats(par, (lambda lo, hi: o2[:, lo:hi]), D, EPS_LN)
                    p.wait("scalar", "V", v_ln4)
                    SC(lambda e, s, par=par: e.activation(out=a2[:], in_=o2[:], func=AF.Identity,
                                                          scale=stat[par][:, 19:20], bias=stat[par][:, 21:22]), inc=True)
                    sc_ln4 = p.cnt["SC"]
                    p.wait("vector", "SC", sc_ln4)
                    p.wait("vector", "LD", ld_x1b)
                    p.wait("vector", "G", y_st[par])
                    V(lambda e, s, par=par: e.tensor_tensor(out=yst[par][:], in0=a2[:], in1=x1c[par][:], op=AluOpType.add), inc=True)
                    lastread[("x1c2", par)] = [("V", p.cnt["V"])]
                    v_y = p.cnt["V"]
                    p.wait("gpsimd", "V", v_y)
                    y_st[par] = STORE(lambda e, s, tg=tg, par=par: e.dma_start(out=y_out[tg * 128:(tg + 1) * 128, :], in_=yst[par][:]))

            p.wait("gpsimd", "G", p.cnt["G"])

            block = st3.enter_context(nc.Block())
            p.realize(block, sems)
    return nc


_NC_CACHE = None
DEBUG = False
QK_PASSES = 3


def kernel(L0_attn_inputs, L0_attn_Wq, L0_attn_Wk, L0_attn_Wv, L0_attn_Wo,
           L0_attn_q_norm_w, L0_attn_k_norm_w, L0_ffn_w1, L0_ffn_conv_w,
           L0_ffn_conv_b, L0_ffn_w2):
    global _NC_CACHE
    x = np.ascontiguousarray(np.asarray(L0_attn_inputs, dtype=np.float32))
    B = x.shape[0]
    wq = np.ascontiguousarray(np.asarray(L0_attn_Wq, dtype=np.float32))
    wk = np.ascontiguousarray(np.asarray(L0_attn_Wk, dtype=np.float32))
    wv = np.ascontiguousarray(np.asarray(L0_attn_Wv, dtype=np.float32))
    wo = np.ascontiguousarray(np.asarray(L0_attn_Wo, dtype=np.float32))
    qn = np.ascontiguousarray(np.asarray(L0_attn_q_norm_w, dtype=np.float32)).reshape(1, DH)
    kn = np.ascontiguousarray(np.asarray(L0_attn_k_norm_w, dtype=np.float32)).reshape(1, DH)
    w1a = np.ascontiguousarray(np.asarray(L0_ffn_w1, dtype=np.float32))
    cwa = np.ascontiguousarray(np.asarray(L0_ffn_conv_w, dtype=np.float32)).reshape(DHID, 3)
    cba = np.ascontiguousarray(np.asarray(L0_ffn_conv_b, dtype=np.float32)).reshape(1, DHID)
    w2a = np.ascontiguousarray(np.asarray(L0_ffn_w2, dtype=np.float32))

    if _NC_CACHE is None:
        _NC_CACHE = build_nc()
    nc = _NC_CACHE
    in_maps = []
    for b in range(B):
        in_maps.append({"x": np.ascontiguousarray(x[b]), "Wq": wq, "Wk": wk,
                        "Wv": wv, "Wo": wo, "qnw": qn, "knw": kn, "w1": w1a,
                        "cw": cwa, "cb": cba, "w2": w2a})
    res = run_bass_kernel_spmd(nc, in_maps, core_ids=list(range(B)))
    out = np.stack([np.asarray(res.results[b]["y"]) for b in range(B)], axis=0)
    return out.astype(np.float32)
